# version 12
# baseline (speedup 1.0000x reference)
import numpy as np
import ml_dtypes
from contextlib import ExitStack

import concourse.bass as bass
import concourse.mybir as mybir
from concourse.bass_utils import run_bass_kernel_spmd

F32 = mybir.dt.float32
BF16 = mybir.dt.bfloat16
AF = mybir.ActivationFunctionType
ALU = mybir.AluOpType
AX = mybir.AxisListType
BFNP = ml_dtypes.bfloat16

T = 4096
D = 1024
DI = 2048
DS = 16
INNER = 4096
C = 512
NCH = T // C
NTOK = C // 128
MAGIC = 12582912.0
LN_EPS = 1e-5

ENGINES = ("tensor", "vector", "scalar", "gpsimd", "sync")

_DBG = set()
_STAGE = 99
_PREP = 9


class Sched:
    def __init__(self, nc, es):
        self.nc = nc
        self.q = {e: [] for e in ENGINES}
        self.cnt = {}
        self.sems = {}
        for name in ("P", "V", "S", "G", "DW", "DX"):
            self.sems[name] = es.enter_context(nc.semaphore(name))
            self.cnt[name] = 0

    def emit(self, eng, fn, waits=(), inc=None, inc_n=1):
        w = tuple((s, v) for (s, v) in waits if v > 0)
        if inc is not None:
            self.cnt[inc] += inc_n
        self.q[eng].append((fn, w, inc, inc_n))
        return self.cnt[inc] if inc is not None else None

    def val(self, s):
        return self.cnt[s]

    def realize(self, block):
        sems = self.sems
        for eng in ENGINES:
            ops = self.q[eng]

            def make(ops):
                def body(e):
                    last = {}
                    for fn, waits, inc, inc_n in ops:
                        for s, v in waits:
                            if v > last.get(s, 0):
                                e.wait_ge(sems[s], v)
                                last[s] = v
                        inst = fn(e)
                        if inc is not None:
                            if inst is None:
                                inst = e.nop()
                            inst.then_inc(sems[inc], inc_n)
                return body

            getattr(block, eng)(make(ops))


def build_program():
    nc = bass.Bass("TRN2", target_bir_lowering=False)
    es = ExitStack()

    dx = nc.dram_tensor("x", [T, D], F32, kind="ExternalInput").ap()
    d_nw = nc.dram_tensor("norm_w", [D], F32, kind="ExternalInput").ap()
    d_nb = nc.dram_tensor("norm_b", [D], F32, kind="ExternalInput").ap()
    d_win = nc.dram_tensor("W_in", [D, DI], F32, kind="ExternalInput").ap()
    d_wgate = nc.dram_tensor("W_gate", [D, DI], F32, kind="ExternalInput").ap()
    d_wdt = nc.dram_tensor("W_dt", [DI, DI], F32, kind="ExternalInput").ap()
    d_bdt = nc.dram_tensor("b_dt", [DI], F32, kind="ExternalInput").ap()
    d_wB = nc.dram_tensor("W_B", [DI, DS], F32, kind="ExternalInput").ap()
    d_wC = nc.dram_tensor("W_C", [DI, DS], F32, kind="ExternalInput").ap()
    d_alog = nc.dram_tensor("A_log", [DI, DS], F32, kind="ExternalInput").ap()
    d_wout = nc.dram_tensor("W_out", [DI, D], F32, kind="ExternalInput").ap()
    d_fg = nc.dram_tensor("ffn_w_gate", [INNER, D], F32, kind="ExternalInput").ap()
    d_fu = nc.dram_tensor("ffn_w_up", [INNER, D], F32, kind="ExternalInput").ap()
    d_fd = nc.dram_tensor("ffn_w_down", [D, INNER], F32, kind="ExternalInput").ap()
    d_idbf = nc.dram_tensor("ident_bf", [128, 128], BF16, kind="ExternalInput").ap()
    d_idf = nc.dram_tensor("ident_f32", [128, 128], F32, kind="ExternalInput").ap()
    d_ones_r_bf = nc.dram_tensor("ones_r_bf", [1, 128], BF16, kind="ExternalInput").ap()
    d_ones_r_f = nc.dram_tensor("ones_r_f", [1, 128], F32, kind="ExternalInput").ap()
    d_ones_c_f = nc.dram_tensor("ones_c_f", [128, 1], F32, kind="ExternalInput").ap()
    d_sel = nc.dram_tensor("sel32", [32, 32 * 128], BF16, kind="ExternalInput").ap()

    d_out = nc.dram_tensor("out", [T, D], F32, kind="ExternalOutput").ap()

    s_win = nc.dram_tensor("s_win", [D, DI], BF16, kind="Internal").ap()
    s_wgate = nc.dram_tensor("s_wgate", [D, DI], BF16, kind="Internal").ap()
    s_wdt = nc.dram_tensor("s_wdt", [DI, DI], BF16, kind="Internal").ap()
    s_wout = nc.dram_tensor("s_wout", [DI, D], BF16, kind="Internal").ap()
    s_fgT = nc.dram_tensor("s_fgT", [D, INNER], BF16, kind="Internal").ap()
    s_fuT = nc.dram_tensor("s_fuT", [D, INNER], BF16, kind="Internal").ap()
    s_fdT = nc.dram_tensor("s_fdT", [INNER, D], BF16, kind="Internal").ap()

    dbg_outs = {}
    for name, shape, dt in [
        ("xn0_tm", [128, NTOK, D], BF16), ("xn_fm", [128, 8, C], BF16),
        ("z_fm", [128, 16, C], BF16), ("dt_fm", [128, 16, C], BF16),
        ("gate_fm", [128, 16, C], BF16), ("bc_sb", [32, C], BF16),
        ("y_fm", [128, 16, C], BF16), ("attn_fm", [128, 8, C], BF16),
        ("x1_tm", [128, NTOK, D], F32), ("xhat_fm", [128, 8, C], BF16),
        ("gu_fm", [128, 32, C], BF16), ("x2hat_fm", [128, 32, C], BF16),
        ("wq_probe", [128, 512], BF16), ("a_probe", [128, 16, 16], F32),
        ("tmA", [128, NTOK, D], BF16), ("wq2_probe", [128, 512], BF16),
        ("f_fm", [128, 8, C], BF16),
        ("scales_probe", [128, 8], F32),
    ]:
        if name in _DBG:
            dbg_outs[name] = nc.dram_tensor("dbg_" + name, shape, dt,
                                            kind="ExternalOutput").ap()

    sch = Sched(nc, es)
    P_, V_, S_, G_, DW, DX = "P", "V", "S", "G", "DW", "DX"

    import numpy as _np
    _alloc_log = []

    def SBe(e_, n, s, d):
        bpp = int(_np.prod(s[1:])) * mybir.dt.size(d)
        _alloc_log.append((n, bpp))
        return e_.enter_context(nc.sbuf_tensor(n, s, d))
    SB = lambda n, s, d: SBe(es, n, s, d)
    build_program._alloc_log = _alloc_log
    PSUM = lambda n, s, d: es.enter_context(nc.psum_tensor(n, s, d))

    idbf = SB("sb_idbf", [128, 128], BF16)
    idf = SB("sb_idf", [128, 128], F32)
    ones_r_bf = SB("sb_ones_r_bf", [1, 128], BF16)
    ones_r_f = SB("sb_ones_r_f", [1, 128], F32)
    ones_c_f = SB("sb_ones_c_f", [128, 1], F32)
    sel = SB("sb_sel", [32, 32 * 128], BF16)
    nw_rep = SB("nw_rep", [128, D], BF16)
    nb_rep = SB("nb_rep", [128, D], BF16)
    a_sb = SB("a_sb", [128, 16, 16], F32)
    bdt_sb = SB("bdt_sb", [128, 16], F32)
    wbc = SB("wbc", [128, 16, 32], BF16)
    wgs_bc = SB("wgs_bc", [128, 1], F32)
    wus_bc = SB("wus_bc", [128, 1], F32)
    wds_bc = SB("wds_bc", [128, 1], F32)
    wgr_bc = SB("wgr_bc", [128, 1], F32)
    wur_bc = SB("wur_bc", [128, 1], F32)
    wdr_bc = SB("wdr_bc", [128, 1], F32)
    H = SB("H", [128, 256], F32)
    magic_col = SB("magic_col", [128, 1], F32)
    eps_col = SB("eps_col", [128, 1], F32)
    one_col = SB("one_col", [128, 1], F32)
    neghalf_col = SB("neghalf_col", [128, 1], F32)
    negmagic_col = SB("negmagic_col", [128, 1], F32)

    NSLOT = 2
    SEGSZ = 4096
    wslot = [SB(f"wslot{j}", [128, SEGSZ], BF16) for j in range(NSLOT)]

    psROT = [PSUM(f"ps{j}", [128, 512], F32) for j in range(5)]
    psTR = PSUM("psTR", [128, 1024], BF16)
    psTRf = PSUM("psTRf", [128, 512], F32)

    block = es.enter_context(nc.Block())

    class Rot:
        def __init__(self, banks):
            self.banks = banks
            self.i = 0
            self.guard = [("V", 0)] * len(banks)

        def next(self):
            j = self.i % len(self.banks)
            self.i += 1
            return j, self.banks[j], [self.guard[j]]

        def set_guard(self, j, sem, val):
            self.guard[j] = (sem, val)

    rot = Rot(psROT)
    trg = {"tr": ("V", 0), "trf": ("V", 0)}

    def pace(eng="tensor"):
        sch.emit(eng, lambda e: e.wait_ge(sch.sems["DW"], 16))

    def dma(queue, out_ap, in_ap, sem, waits=()):
        sch.emit(queue, lambda e, o=out_ap, i=in_ap: e.dma_start(out=o, in_=i),
                 waits=waits, inc=sem, inc_n=16)
        return sch.val(sem)

    for dst, src in [(idbf, d_idbf), (idf, d_idf), (ones_r_bf, d_ones_r_bf),
                     (ones_r_f, d_ones_r_f), (ones_c_f, d_ones_c_f),
                     (sel, d_sel)]:
        dma("sync", dst[:], src[:], DW)
    consts_dw = dma("sync", a_sb[:], d_alog.rearrange("(dc p) s -> p dc s", p=128), DW)

    s0 = sch.emit("scalar", lambda e: e.activation(out=a_sb[:], in_=a_sb[:],
                  func=AF.Exp), waits=[(DW, consts_dw)], inc=S_)
    va = sch.emit("vector", lambda e: e.tensor_scalar(out=a_sb[:], in0=a_sb[:],
             scalar1=1e-6, scalar2=1.0, op0=ALU.max, op1=ALU.min),
             waits=[(S_, s0)], inc=V_)
    sch.emit("vector", lambda e, v=va: e.wait_ge(sch.sems["V"], v))
    a_ready = sch.emit("vector", lambda e: e.tensor_scalar(out=a_sb[:],
                       in0=a_sb[:], scalar1=-1.0, scalar2=None, op0=ALU.mult),
                       inc=V_)
    sch.emit("vector", lambda e: e.memset(H[:], 0.0), inc=V_)
    bdt_row = SB("bdt_row", [16, 128], F32)
    bdtw = dma("sync", bdt_row[:], d_bdt.rearrange("(k p) -> k p", p=128), DW)
    bdt_tp = sch.emit("tensor", lambda e: e.transpose(
        psTRf[:, 0:16], bdt_row[:], idf[0:16, 0:16]), waits=[(DW, bdtw)], inc=P_)
    bdt_done = sch.emit("vector", lambda e: e.tensor_copy(
        out=bdt_sb[:], in_=psTRf[:, 0:16]), waits=[(P_, bdt_tp)], inc=V_)
    trg["trf"] = ("V", bdt_done)
    sch.emit("vector", lambda e: e.memset(magic_col[:], MAGIC), inc=None)
    sch.emit("vector", lambda e: e.memset(eps_col[:], LN_EPS), inc=None)
    sch.emit("vector", lambda e: e.memset(one_col[:], 1.0), inc=None)
    sch.emit("vector", lambda e: e.memset(neghalf_col[:], -0.5), inc=None)
    sch.emit("vector", lambda e: e.memset(negmagic_col[:], -MAGIC), inc=V_)

    es_prep = ExitStack()
    nrow_f = SBe(es_prep, "nrow_f", [1, D], F32)
    nrow_b = SBe(es_prep, "nrow_b", [1, D], BF16)
    wbc_f = SBe(es_prep, "wbc_f", [128, 16, 16], F32)
    wtmp_f = [SBe(es_prep, f"wtmp_f{j}", [128, 4096], F32) for j in range(2)]
    wtmp_b = [SBe(es_prep, f"wtmp_b{j}", [128, 4096], BF16) for j in range(2)]
    sabs = SBe(es_prep, "sabs", [128, 2], F32)
    pr = SBe(es_prep, "pr", [128, 2, 512], BF16)
    sc8 = SBe(es_prep, "sc8", [1, 8], F32)

    rep_pm = 0
    for vec_d, rep in ([(d_nw, nw_rep), (d_nb, nb_rep)] if _PREP >= 1 else []):
        w = dma("sync", nrow_f[:], vec_d[None, :], DW,
                waits=[(V_, sch.val(V_))])
        cv = sch.emit("vector", lambda e: e.tensor_copy(out=nrow_b[:], in_=nrow_f[:]),
                      waits=[(DW, w), (P_, rep_pm)], inc=V_)
        for ns in range(2):
            j, ps, g = rot.next()
            rep_pm = sch.emit("tensor", lambda e, ps=ps, ns=ns: e.matmul(
                ps[:], ones_r_bf[:], nrow_b[0:1, ns * 512:(ns + 1) * 512],
                start=True, stop=True), waits=[(V_, cv)] + g, inc=P_)
            ev = sch.emit("vector", lambda e, ps=ps, rep=rep, ns=ns: e.tensor_copy(
                out=rep[:, ns * 512:(ns + 1) * 512], in_=ps[:]),
                waits=[(P_, rep_pm)], inc=V_)
            rot.set_guard(j, V_, ev)

    for col0, src in ([(0, d_wB), (16, d_wC)] if _PREP >= 1 else []):
        w = dma("sync", wbc_f[:], src.rearrange("(k p) s -> p k s", p=128), DW,
                waits=[(V_, sch.val(V_))])
        sch.emit("vector", lambda e, c0=col0: e.tensor_copy(
            out=wbc[:, :, c0:c0 + 16], in_=wbc_f[:]), waits=[(DW, w)], inc=V_)
    wbc_ready = sch.val(V_)

    prep_cast_guard = [("V", 0), ("V", 0)]

    def cast_weight(src, dst, rows, cols):
        nb = min(4096 // cols, rows // 128)
        segrows = nb * 128
        for sgi in range(rows // segrows):
            r0 = sgi * segrows
            h = sgi % 2
            wtf = wtmp_f[h][:, 0:nb * cols].rearrange("p (a b) -> p a b", a=nb)
            wtb = wtmp_b[h][:, 0:nb * cols].rearrange("p (a b) -> p a b", a=nb)
            w = dma("sync", wtf,
                    src[r0:r0 + segrows, :].rearrange("(k p) m -> p k m", p=128),
                    DW, waits=[prep_cast_guard[h]])
            cv = sch.emit("vector", lambda e, wtf=wtf, wtb=wtb: e.tensor_copy(
                out=wtb, in_=wtf), waits=[(DW, w)], inc=V_)
            prep_cast_guard[h] = (V_, cv)
            dma("sync",
                dst[r0:r0 + segrows, :].rearrange("(k p) m -> p k m", p=128),
                wtb, DW, waits=[(V_, cv)])

    if _PREP >= 2:
        cast_weight(d_win, s_win, D, DI)
        cast_weight(d_wgate, s_wgate, D, DI)
        cast_weight(d_wdt, s_wdt, DI, DI)
        cast_weight(d_wout, s_wout, DI, D)

    qt_state = {"pr_guard": [("V", 0), ("V", 0)], "prslot": [0],
                "tpg": [("P", 0), ("P", 0)]}

    def quant_transpose(src, dstT, rows, cols, s_bc, r_bc, si, p2=3):
        last_tp = sch.val(P_)
        nb = min(4096 // cols, rows // 128)
        segrows = nb * 128
        for sgi in range(rows // segrows):
            r0 = sgi * segrows
            h = sgi % 2
            wtf = wtmp_f[h][:, 0:nb * cols].rearrange("p (a b) -> p a b", a=nb)
            w = dma("sync", wtf,
                    src[r0:r0 + segrows, :].rearrange("(k p) m -> p k m", p=128),
                    DW, waits=[prep_cast_guard[h]])
            if sgi == 0:
                vr = sch.emit("vector", lambda e, wtf=wtf: e.reduce_sum(
                    out=sabs[:, 0:1], in_=wtf, axis=AX.XY,
                    apply_absolute_value=True), waits=[(DW, w)], inc=V_)
                prep_cast_guard[h] = (V_, vr)
            else:
                vr = sch.emit("vector", lambda e, wtf=wtf: e.reduce_sum(
                    out=sabs[:, 1:2], in_=wtf, axis=AX.XY,
                    apply_absolute_value=True), waits=[(DW, w)], inc=V_)
                prep_cast_guard[h] = (V_, vr)
                sch.emit("vector", lambda e, v=vr: e.wait_ge(sch.sems["V"], v))
                sch.emit("vector", lambda e: e.tensor_add(
                    out=sabs[:, 0:1], in0=sabs[:, 0:1], in1=sabs[:, 1:2]), inc=V_)
        pm = sch.emit("tensor", lambda e: e.transpose(
            psTRf[0:1, 0:128], sabs[:, 0:1], idf[:]),
            waits=[(V_, sch.val(V_)), trg["trf"]], inc=P_)
        sl = sc8[0:1, si:si + 1]
        sl2 = sc8[0:1, si + 1:si + 2]
        vr = sch.emit("vector", lambda e: e.reduce_sum(
            out=sabs[0:1, 1:2], in_=psTRf[0:1, 0:128], axis=AX.X),
            waits=[(P_, pm)], inc=V_)
        sch.emit("vector", lambda e, v=vr: e.wait_ge(sch.sems["V"], v))
        ev = sch.emit("vector", lambda e, sl=sl: e.tensor_scalar(
            out=sl, in0=sabs[0:1, 1:2], scalar1=1.0 / (rows * cols),
            scalar2=1e-5, op0=ALU.mult, op1=ALU.max), inc=V_)
        trg["trf"] = (V_, ev)
        sch.emit("vector", lambda e, v=ev: e.wait_ge(sch.sems["V"], v))
        sch.emit("vector", lambda e, sl=sl, sl2=sl2: e.reciprocal(
            out=sl2, in_=sl), inc=V_)
        vv = sch.val(V_)
        sch.emit("vector", lambda e, v=vv: e.wait_ge(sch.sems["V"], v))
        for val_ap, bc in [(sl, s_bc), (sl2, r_bc)]:
            j, ps, g = rot.next()
            pm = sch.emit("tensor", lambda e, ps=ps, v=val_ap: e.matmul(
                ps[:, 0:1], ones_r_f[:], v, start=True, stop=True),
                waits=[(V_, vv)] + g, inc=P_)
            ev = sch.emit("vector", lambda e, ps=ps, bc=bc: e.tensor_copy(
                out=bc[:], in_=ps[:, 0:1]), waits=[(P_, pm)], inc=V_)
            rot.set_guard(j, V_, ev)
        scales_v = sch.val(V_)
        if p2 == 0:
            return
        pr_guard = qt_state["pr_guard"]
        prslot = qt_state["prslot"]
        for sgi in range(rows // segrows):
            r0 = sgi * segrows
            h = sgi % 2
            wtf = wtmp_f[h][:, 0:nb * cols].rearrange("p (a b) -> p a b", a=nb)
            wtb = wtmp_b[h][:, 0:nb * cols].rearrange("p (a b) -> p a b", a=nb)
            w = dma("sync", wtf,
                    src[r0:r0 + segrows, :].rearrange("(k p) m -> p k m", p=128),
                    DW, waits=[prep_cast_guard[h]])
            sq = sch.emit("scalar", lambda e, wtf=wtf: e.activation(
                out=wtf, in_=wtf, func=AF.Identity,
                scale=r_bc[:, 0:1], bias=magic_col[:, 0:1]),
                waits=[(DW, w), (V_, scales_v)], inc=S_)
            sch.emit("vector", lambda e, wtf=wtf: e.tensor_scalar(
                out=wtf, in0=wtf, scalar1=-MAGIC,
                scalar2=-1.0, op0=ALU.add, op1=ALU.max),
                waits=[(S_, sq), qt_state["tpg"][h]], inc=None)
            cv = sch.emit("vector", lambda e, wtf=wtf, wtb=wtb: e.tensor_scalar(
                out=wtb, in0=wtf, scalar1=1.0,
                scalar2=None, op0=ALU.min), inc=V_)
            prep_cast_guard[h] = (V_, cv)
            if p2 == 1:
                continue
            for jb in range(cols // 128):
                for ibg in range((nb + 3) // 4):
                    nbl = min(4, nb - ibg * 4)
                    ltp = None
                    for ib2 in range(nbl):
                        ib = ibg * 4 + ib2
                        w = ([(V_, cv), trg["tr"]] if ib2 == 0
                             else [(P_, ltp)])
                        ltp = sch.emit("tensor", lambda e, ib=ib, ib2=ib2,
                                       jb=jb, wtb=wtb: e.transpose(
                            psTR[:, ib2 * 128:(ib2 + 1) * 128],
                            wtb[:, ib, jb * 128:(jb + 1) * 128], idbf[:]),
                            waits=w, inc=P_)
                    last_tp = ltp
                    qt_state["tpg"][h] = (P_, ltp)
                    slot = prslot[0] % 2
                    prslot[0] += 1
                    ev = sch.emit("vector", lambda e, slot=slot, nbl=nbl:
                                  e.tensor_copy(
                        out=pr[:, slot, 0:nbl * 128], in_=psTR[:, 0:nbl * 128]),
                        waits=[(P_, last_tp), pr_guard[slot]], inc=V_)
                    trg["tr"] = (V_, ev)
                    if p2 == 2:
                        pr_guard[slot] = (V_, ev)
                        continue
                    c0 = r0 + ibg * 512
                    st = dma("sync",
                             dstT[jb * 128:(jb + 1) * 128, c0:c0 + nbl * 128],
                             pr[:, slot, 0:nbl * 128], DW, waits=[(V_, ev)])
                    pr_guard[slot] = (DW, st)

    if _PREP >= 3:
        quant_transpose(d_fg, s_fgT, INNER, D, wgs_bc, wgr_bc, 0,
                        p2={3: 0, 4: 1, 5: 2, 6: 3}.get(_PREP, 3))
    if _PREP >= 7:
        quant_transpose(d_fu, s_fuT, INNER, D, wus_bc, wur_bc, 2)
        quant_transpose(d_fd, s_fdT, D, INNER, wds_bc, wdr_bc, 4)

    if "wq_probe" in dbg_outs and _PREP >= 6:
        w = dma("sync", pr[:, 0, :], s_fgT[0:128, 0:512], DW,
                waits=[(DW, sch.val(DW))])
        dma("sync", dbg_outs["wq_probe"][:], pr[:, 0, :], DX, waits=[(DW, w)])
    if "wq2_probe" in dbg_outs and _PREP >= 7:
        w = dma("sync", pr[:, 1, :], s_fdT[0:128, 0:512], DW,
                waits=[(DW, sch.val(DW))])
        dma("sync", dbg_outs["wq2_probe"][:], pr[:, 1, :], DX, waits=[(DW, w)])
    if "a_probe" in dbg_outs:
        dma("sync", dbg_outs["a_probe"][:], a_sb[:], DX, waits=[(V_, a_ready)])
    if "scales_probe" in dbg_outs and _PREP >= 3:
        pass
    if "scales_probe" in dbg_outs and _PREP >= 3:
        sp_b = SBe(es_prep, "sp_b", [128, 8], F32)
        for jj, bc in enumerate([wgs_bc, wgr_bc, wus_bc, wur_bc, wds_bc, wdr_bc]):
            sch.emit("vector", lambda e, jj=jj, bc=bc: e.tensor_copy(
                out=sp_b[:, jj:jj + 1], in_=bc[:]), inc=V_)
        dma("sync", dbg_outs["scales_probe"][:], sp_b[:], DX,
            waits=[(V_, sch.val(V_))])

    es_prep.close()
    prep_v = sch.val(V_)
    prep_s = sch.val(S_)
    prep_p = sch.val(P_)
    prep_dw = sch.val(DW)

    x_tmd = SB("x_tm", [128, 2, NTOK, D], F32)
    xn0_tmd = SB("xn0_tm", [128, 2, NTOK, D], BF16)
    tmA = SB("tmA", [128, 2, D], BF16)
    tmB = SB("tmB", [128, 2, D], BF16)
    xn_fm = SB("xn_fm", [128, 8, C], BF16)
    fmS = SB("fmS", [128, 8, C], BF16)
    gu_fm = SB("gu_fm", [128, 32, C], BF16)
    z_fm = gu_fm[:, 0:16, :]
    dt_fm = gu_fm[:, 16:32, :]
    gate_fm = SB("gate_fm", [128, 16, C], BF16)
    y_fm = SB("y_fm", [128, 16, C], BF16)
    f_fm = SB("f_fm", [128, 8, C], BF16)
    bc_sb = SB("bc_sb", [32, C], BF16)
    b_bc = SB("b_bc", [128, C], BF16)
    c_bc = SB("c_bc", [128, C], BF16)
    dAr = SB("dAr", [128, 2, C], BF16)
    br_s = SB("br", [128, C], BF16)
    hr = SB("hr", [128, 2, C], BF16)
    cmr2 = SB("cmr2", [128, 2, C], BF16)
    sp_r = SB("sp_r", [128, 1, C], BF16)
    dump = SB("dump", [128, D], BF16)
    f32r = [SB(f"f32r{j}", [128, C], F32) for j in range(2)]
    mhi = SB("mhi", [128, C], BF16)
    mlo = SB("mlo", [128, C], BF16)
    invsx_bc = SB("invsx_bc", [128, C], F32)
    comb_bc = SB("comb_bc", [128, C], F32)
    unsc_bc = SB("unsc_bc", [128, C], F32)
    row_f = SB("row_f", [1, 3 * C], BF16)
    stat = SB("stat", [128, 48], F32)
    statb = SB("statb", [128, 4], F32)
    sg_b = SB("sg_b", [128, C], BF16)

    seg_state = {"n": 0, "free": [("P", 0)] * NSLOT}

    def load_seg(dram_view):
        n = seg_state["n"]
        slot = n % NSLOT
        a = dram_view.shape[1]
        b = dram_view.shape[2]
        dst = wslot[slot][:, 0:a * b].rearrange("p (a b) -> p a b", a=a)
        w = dma("sync", dst, dram_view, DW,
                waits=[seg_state["free"][slot], (DW, prep_dw)])
        seg_state["n"] = n + 1
        return slot, w

    def seg_view(slot, a, b):
        return wslot[slot][:, 0:a * b].rearrange("p (a b) -> p a b", a=a)

    def seg_done(slot, sem, val):
        seg_state["free"][slot] = (sem, val)

    ln_pool = {"i": 0, "guard": [("V", 0)] * 4}

    def ln_norm(src_ap, out_ap, extra_waits=()):
        slot = ln_pool["i"] % 4
        ln_pool["i"] += 1
        c = 3 * slot
        bncol = stat[:, 24 + 12 * (slot % 2):24 + 12 * (slot % 2) + 12]
        mv = stat[:, c:c + 2]
        c2 = stat[:, c + 2:c + 3]
        bias = statb[:, slot:slot + 1]
        ew = list(extra_waits) + [ln_pool["guard"][slot]]
        sch.emit("vector", lambda e: e.bn_stats(
            out=bncol[:, 0:6], in_=src_ap[:, 0:512]), waits=ew, inc=V_)
        v1 = sch.emit("vector", lambda e: e.bn_stats(
            out=bncol[:, 6:12], in_=src_ap[:, 512:1024]), inc=V_)
        sch.emit("vector", lambda e: e.wait_ge(sch.sems["V"], v1))
        v2 = sch.emit("vector", lambda e: e.bn_aggr(
            out=mv, in_=bncol), inc=V_)
        sch.emit("vector", lambda e: e.wait_ge(sch.sems["V"], v2))
        v3 = sch.emit("vector", lambda e: e.tensor_scalar(
            out=mv[:, 0:1], in0=mv[:, 0:1], scalar1=-1.0, scalar2=None,
            op0=ALU.mult), inc=V_)
        s1 = sch.emit("scalar", lambda e: e.activation(
            out=c2, in_=mv[:, 1:2], func=AF.Ln, bias=eps_col[:, 0:1]),
            waits=[(V_, v2)], inc=S_)
        sch.emit("scalar", lambda e: e.wait_ge(sch.sems["S"], s1))
        s2 = sch.emit("scalar", lambda e: e.activation(
            out=c2, in_=c2, func=AF.Exp, scale=neghalf_col[:, 0:1]), inc=S_)
        sch.emit("scalar", lambda e: e.wait_ge(sch.sems["S"], s2))
        s3 = sch.emit("scalar", lambda e: e.activation(
            out=bias, in_=mv[:, 0:1], func=AF.Identity, scale=c2),
            waits=[(V_, v3)], inc=S_)
        sch.emit("scalar", lambda e: e.wait_ge(sch.sems["S"], s3))
        ap_s = sch.emit("scalar", lambda e: e.activation(
            out=out_ap, in_=src_ap, func=AF.Identity, scale=c2, bias=bias),
            inc=S_)
        ln_pool["guard"][slot] = (S_, ap_s)
        return ap_s

    def fm_mm(wsrc, rows, out_cols, rhs_fm, evac, rhs_ready, seg_mt=4,
              extra_mm_waits=()):
        K = rows // 128
        for sg in range(out_cols // (seg_mt * 128)):
            m0 = sg * seg_mt * 128
            slot, w = load_seg(wsrc[:, m0:m0 + seg_mt * 128].rearrange(
                "(k p) m -> p k m", p=128))
            sv = seg_view(slot, K, seg_mt * 128)
            lastmm = None
            for mt in range(seg_mt):
                j, ps, g = rot.next()
                for k in range(K):
                    waits = ([(DW, w), (V_, rhs_ready)] + g +
                             list(extra_mm_waits)) if k == 0 else ()
                    lastmm = sch.emit("tensor", lambda e, ps=ps, sv=sv, k=k,
                                      mt=mt, K=K: e.matmul(
                        ps[:], sv[:, k, mt * 128:(mt + 1) * 128],
                        rhs_fm[:, k, :], start=(k == 0), stop=(k == K - 1)),
                        waits=waits, inc=(P_ if k == K - 1 else None))
                sem, val = evac(sg * seg_mt + mt, ps, [(P_, lastmm)])
                rot.set_guard(j, sem, val)
            seg_done(slot, P_, lastmm)

    def transpose_tile(src_ap_fn, dst_fn, ready, n_jg=2, extra_first=(),
                       ev_sem="V"):
        for jg in range(n_jg):
            tp = None
            for jj in range(4):
                j = jg * 4 + jj
                w = ([ready, trg["tr"]] + list(extra_first)) if jj == 0 \
                    else [(P_, tp)]
                tp = sch.emit("tensor", lambda e, j=j, jj=jj, jg=jg: e.transpose(
                    psTR[:, jj * 128 + jg * 512:(jj + 1) * 128 + jg * 512],
                    src_ap_fn(j), idbf[:]), waits=w, inc=P_)
            ev = dst_fn(jg, [(P_, tp)])
            trg["tr"] = (ev_sem, ev)

    out_dx = 0

    st = [dict() for _ in range(NCH)]
    gd = {
        "xpar": [[("V", 0)], [("V", 0)]],
        "x0par": [[("V", 0)], [("V", 0)]],
        "xnfm": [("V", 0)],
        "zdt": [("V", 0)],
        "fmS": [("V", 0)],
        "xstore": [0, 0],
    }
    rngT = {"i": 0, "g": [("V", 0), ("V", 0)]}
    rngB = {"i": 0, "g": [("V", 0), ("V", 0)]}

    def phase_A(ch):
        t0 = ch * C
        par = ch % 2
        x_tm = x_tmd[:, par]
        xn0_tm = xn0_tmd[:, par]
        fw = list(gd["xpar"][par]) + [(DX, gd["xstore"][par])]
        if ch == 0:
            fw += [(V_, prep_v), (S_, prep_s), (P_, prep_p), (DW, prep_dw)]
        xw = dma("scalar", x_tm[:],
                 dx[t0:t0 + C, :].rearrange("(i p) m -> p i m", p=128), DX,
                 waits=fw)
        for i in range(NTOK):
            s1 = ln_norm(x_tm[:, i, :], xn0_tm[:, i, :],
                         extra_waits=[(DX, xw)] + gd["x0par"][par])
            sl = rngT["i"] % 2
            rngT["i"] += 1
            xs = tmA[:, sl, :]
            s2 = ln_norm(xn0_tm[:, i, :], xs, extra_waits=[(S_, s1),
                                                           rngT["g"][sl]])
            sch.emit("vector", lambda e, xs=xs: e.tensor_mul(
                out=xs, in0=xs, in1=nw_rep[:]), waits=[(S_, s2)], inc=None)
            xnv = sch.emit("vector", lambda e, xs=xs: e.tensor_add(
                out=xs, in0=xs, in1=nb_rep[:]), inc=V_)

            def dst_fn(jg, waits, i=i):
                return sch.emit("scalar", lambda e, i=i, jg=jg: e.activation(
                    out=xn_fm[:, jg * 4:(jg + 1) * 4, i * 128:(i + 1) * 128],
                    in_=psTR[:, jg * 512:(jg + 1) * 512].rearrange(
                        "p (a b) -> p a b", a=4), func=AF.Identity),
                    waits=waits, inc=S_)
            tp0 = sch.val(P_)
            transpose_tile(lambda j, xs=xs: xs[:, j * 128:(j + 1) * 128],
                           dst_fn, (V_, xnv), extra_first=gd["xnfm"],
                           ev_sem="S")
            rngT["g"][sl] = (P_, sch.val(P_))
        xnr = sch.emit("vector", lambda e: e.nop(),
                       waits=[(S_, sch.val(S_))], inc=V_)
        st[ch]["xn_ready"] = xnr
        st[ch]["xw"] = xw

    def phase_B(ch):
        xn_ready = st[ch]["xn_ready"]
        zdt_guard = gd["zdt"]

        def z_evac(mt, ps, waits):
            v = sch.emit("scalar", lambda e, mt=mt, ps=ps: e.activation(
                out=z_fm[:, mt, :], in_=ps[:], func=AF.Identity),
                waits=waits + zdt_guard, inc=S_)
            return S_, v
        fm_mm(s_win, D, DI, xn_fm, z_evac, xn_ready, seg_mt=4)
        z_ready_s = sch.val(S_)
        z_ready = sch.emit("vector", lambda e: e.nop(),
                           waits=[(S_, z_ready_s)], inc=V_)

        j, ps, g = rot.next()
        lastmm = None
        for k in range(16):
            waits = ([(V_, z_ready), (V_, wbc_ready)] + g) if k == 0 else ()
            lastmm = sch.emit("tensor", lambda e, ps=ps, k=k: e.matmul(
                ps[0:32, :], wbc[:, k, :], z_fm[:, k, :],
                start=(k == 0), stop=(k == 15)),
                waits=waits, inc=(P_ if k == 15 else None))
        bc_ready = sch.emit("vector", lambda e, ps=ps: e.tensor_copy(
            out=bc_sb[:], in_=ps[0:32, :]), waits=[(P_, lastmm)], inc=V_)
        rot.set_guard(j, V_, bc_ready)

        sp_guard = {0: ("V", 0)}
        dt_rdy = [0] * 16

        def dt_evac(mt, ps, waits):
            s1 = sch.emit("scalar", lambda e, ps=ps, mt=mt: e.activation(
                out=sp_r[:, 0, :], in_=ps[:], func=AF.Exp,
                bias=bdt_sb[:, mt:mt + 1]), waits=waits + [sp_guard[0]], inc=S_)
            s2 = sch.emit("scalar", lambda e: e.activation(
                out=sp_r[:, 0, :], in_=sp_r[:, 0, :], func=AF.Ln,
                bias=one_col[:, 0:1]), inc=S_)
            v = sch.emit("vector", lambda e, mt=mt: e.tensor_scalar(
                out=dt_fm[:, mt, :], in0=sp_r[:, 0, :], scalar1=0.001,
                scalar2=1.0, op0=ALU.max, op1=ALU.min),
                waits=[(S_, s2)] + zdt_guard, inc=V_)
            sp_guard[0] = (V_, v)
            dt_rdy[mt] = v
            return S_, s1
        fm_mm(s_wdt, DI, DI, z_fm, dt_evac, z_ready, seg_mt=2)
        dt_ready = sch.val(V_)
        st[ch]["dt_rdy"] = dt_rdy

        def gate_evac(mt, ps, waits):
            s = sch.emit("scalar", lambda e, ps=ps, mt=mt: e.activation(
                out=gate_fm[:, mt, :], in_=ps[:], func=AF.Silu),
                waits=waits, inc=S_)
            return S_, s
        fm_mm(s_wgate, D, DI, xn_fm, gate_evac, xn_ready, seg_mt=4)
        gate_ready_s = sch.val(S_)
        gd["xnfm"] = [(P_, sch.val(P_))]

        st[ch].update(z_ready=z_ready, dt_ready=dt_ready, bc_ready=bc_ready,
                      gate_ready_s=gate_ready_s)
        if ch == 0 and "z_fm" in dbg_outs:
            dma("sync", dbg_outs["z_fm"][:], z_fm, DX, waits=[(V_, z_ready)])
        if ch == 0 and "dt_fm" in dbg_outs:
            dma("sync", dbg_outs["dt_fm"][:], dt_fm, DX, waits=[(V_, dt_ready)])
        if ch == 0 and "gate_fm" in dbg_outs:
            dma("sync", dbg_outs["gate_fm"][:], gate_fm[:], DX,
                waits=[(S_, gate_ready_s)])
        if ch == 0 and "bc_sb" in dbg_outs:
            dma("sync", dbg_outs["bc_sb"][:], bc_sb[:], DX,
                waits=[(V_, bc_ready)])
        if ch == 0 and "xn_fm" in dbg_outs:
            dma("sync", dbg_outs["xn_fm"][:], xn_fm[:], DX,
                waits=[(V_, xn_ready)])

    def phase_scan(ch):
        dt_rdy = st[ch]["dt_rdy"]
        bc_ready = st[ch]["bc_ready"]
        u_rdy = [0] * 16
        for dc in range(16):
            u_rdy[dc] = sch.emit("vector", lambda e, dc=dc: e.tensor_mul(
                out=z_fm[:, dc, :], in0=z_fm[:, dc, :], in1=dt_fm[:, dc, :]),
                waits=[(V_, dt_rdy[dc])], inc=V_)

        NGP = 0
        pend_carry = []
        cm_guard = [("V", 0), ("V", 0)]
        ring_g = [[("V", 0)], [("V", 0)]]
        for s in range(16):
            prev_sweep = [(V_, sch.val(V_)), (G_, sch.val(G_))]
            j, ps, g = rot.next()
            pm = sch.emit("tensor", lambda e, ps=ps, s=s: e.matmul(
                ps[:], sel[:, s * 128:(s + 1) * 128], bc_sb[:],
                start=True, stop=True), waits=[(V_, bc_ready)] + g, inc=P_)
            bv = sch.emit("scalar", lambda e, ps=ps: e.activation(
                out=b_bc[:], in_=ps[:], func=AF.Identity),
                waits=[(P_, pm)] + prev_sweep, inc=S_)
            rot.set_guard(j, S_, bv)
            j, ps, g = rot.next()
            pm = sch.emit("tensor", lambda e, ps=ps, s=s: e.matmul(
                ps[:], sel[:, (16 + s) * 128:(17 + s) * 128], bc_sb[:],
                start=True, stop=True), waits=g, inc=P_)
            cv = sch.emit("scalar", lambda e, ps=ps: e.activation(
                out=c_bc[:], in_=ps[:], func=AF.Identity),
                waits=[(P_, pm)] + prev_sweep, inc=S_)
            rot.set_guard(j, S_, cv)

            for dc in range(16):
                it = s * 16 + dc
                sl = it % 2
                col = s * 16 + dc
                sa = sch.emit("scalar", lambda e, sl=sl, dc=dc, s=s: e.activation(
                    out=dAr[:, sl, :], in_=dt_fm[:, dc, :], func=AF.Exp,
                    scale=a_sb[:, dc, s:s + 1]),
                    waits=ring_g[sl] + [(V_, dt_rdy[dc]), (V_, a_ready)], inc=S_)
                if len(pend_carry) >= 2:
                    psl, pcol = pend_carry.pop(0)
                    sch.emit("vector", lambda e, psl=psl, pcol=pcol:
                             e.tensor_copy(out=H[:, pcol:pcol + 1],
                                           in_=hr[:, psl, C - 1:C]), inc=None)
                sch.emit("vector", lambda e, sl=sl, dc=dc: e.tensor_mul(
                    out=br_s[:], in0=z_fm[:, dc, :], in1=b_bc[:]),
                    waits=[(V_, u_rdy[dc]), (S_, bv)], inc=None)
                hv = sch.emit("vector", lambda e, sl=sl, col=col:
                              e.tensor_tensor_scan(
                    out=hr[:, sl, :], data0=dAr[:, sl, :], data1=br_s[:],
                    initial=H[:, col:col + 1], op0=ALU.mult, op1=ALU.add),
                    waits=[(S_, sa)], inc=V_)
                pend_carry.append((sl, col))
                if dc < NGP:
                    cm = sch.emit("vector", lambda e, sl=sl: e.tensor_mul(
                        out=cmr2[:, sl, :], in0=hr[:, sl, :], in1=c_bc[:]),
                        waits=[(S_, cv), cm_guard[sl]], inc=V_)
                    if s == 0:
                        gy = sch.emit("gpsimd", lambda e, sl=sl, dc=dc:
                                      e.tensor_copy(
                            out=y_fm[:, dc, :], in_=cmr2[:, sl, :]),
                            waits=[(V_, cm)], inc=G_)
                    else:
                        gy = sch.emit("gpsimd", lambda e, sl=sl, dc=dc:
                                      e.tensor_add(
                            out=y_fm[:, dc, :], in0=y_fm[:, dc, :],
                            in1=cmr2[:, sl, :]), waits=[(V_, cm)], inc=G_)
                    cm_guard[sl] = (G_, gy)
                else:
                    if s == 0:
                        sch.emit("vector", lambda e, sl=sl, dc=dc: e.tensor_mul(
                            out=y_fm[:, dc, :], in0=hr[:, sl, :], in1=c_bc[:]),
                            waits=[(S_, cv)], inc=None)
                    else:
                        sch.emit("vector", lambda e, sl=sl: e.tensor_mul(
                            out=br_s[:], in0=hr[:, sl, :], in1=c_bc[:]),
                            waits=[(S_, cv)], inc=None)
                        sch.emit("vector", lambda e, dc=dc: e.tensor_add(
                            out=y_fm[:, dc, :], in0=y_fm[:, dc, :],
                            in1=br_s[:]), inc=None)
                ring_g[sl] = [(V_, hv)]
        while pend_carry:
            psl, pcol = pend_carry.pop()
            vv_ = sch.val(V_)
            sch.emit("vector", lambda e, v=vv_: e.wait_ge(sch.sems["V"], v))
            sch.emit("vector", lambda e, psl=psl, pcol=pcol: e.tensor_copy(
                out=H[:, pcol:pcol + 1], in_=hr[:, psl, C - 1:C]), inc=None)
        y_ready = sch.emit("vector", lambda e: e.nop(),
                           waits=[(G_, sch.val(G_))], inc=V_)
        st[ch]["y_ready"] = y_ready
        if ch == 0 and "y_fm" in dbg_outs:
            dma("sync", dbg_outs["y_fm"][:], y_fm[:], DX, waits=[(V_, y_ready)])

    def phase_D(ch):
        par = ch % 2
        x_tm = x_tmd[:, par]
        xn0_tm = xn0_tmd[:, par]
        y_ready = st[ch]["y_ready"]
        gate_ready_s = st[ch]["gate_ready_s"]
        for k in range(16):
            sch.emit("vector", lambda e, k=k: e.tensor_mul(
                out=gate_fm[:, k, :], in0=y_fm[:, k, :], in1=gate_fm[:, k, :]),
                waits=([(S_, gate_ready_s), (V_, y_ready)] if k == 0 else ()),
                inc=(V_ if k == 15 else None))
        yg_ready = sch.val(V_)

        def attn_evac(mt, ps, waits):
            v = sch.emit("scalar", lambda e, mt=mt, ps=ps: e.activation(
                out=fmS[:, mt, :], in_=ps[:], func=AF.Identity),
                waits=waits, inc=S_)
            return S_, v
        fm_mm(s_wout, DI, D, gate_fm, attn_evac, yg_ready, seg_mt=2)
        attn_ready_s = sch.val(S_)
        attn_ready = sch.emit("vector", lambda e: e.nop(),
                              waits=[(S_, attn_ready_s)], inc=V_)
        if ch == 0 and "attn_fm" in dbg_outs:
            dma("sync", dbg_outs["attn_fm"][:], fmS[:], DX,
                waits=[(V_, attn_ready)])

        x1v = 0
        for i in range(NTOK):
            sl = rngB["i"] % 2
            rngB["i"] += 1
            rb = tmB[:, sl, :]

            def dst_fn(jg, waits, i=i, rb=rb):
                return sch.emit("vector", lambda e, i=i, jg=jg, rb=rb:
                                e.tensor_add(
                    out=rb[:, jg * 512:(jg + 1) * 512].rearrange(
                        "p (a b) -> p a b", a=4),
                    in0=psTR[:, jg * 512:(jg + 1) * 512].rearrange(
                        "p (a b) -> p a b", a=4),
                    in1=xn0_tm[:, i, jg * 512:(jg + 1) * 512].rearrange(
                        "p (a b) -> p a b", a=4)),
                    waits=waits + ([rngB["g"][sl]] if jg == 0 else []), inc=V_)
            transpose_tile(lambda j, i=i: fmS[:, j, i * 128:(i + 1) * 128],
                           dst_fn, (V_, attn_ready))
            sl2 = rngT["i"] % 2
            rngT["i"] += 1
            ra = tmA[:, sl2, :]
            s3 = ln_norm(rb, ra, extra_waits=[(V_, sch.val(V_)),
                                              rngT["g"][sl2]])
            rngB["g"][sl] = (S_, s3)
            x1v = sch.emit("vector", lambda e, i=i, ra=ra: e.tensor_add(
                out=x_tm[:, i, :], in0=x_tm[:, i, :], in1=ra),
                waits=[(S_, s3)], inc=V_)
            rngT["g"][sl2] = (V_, x1v)
        st[ch]["x1_ready"] = x1v
        gd["x0par"][par] = [(V_, x1v)]
        if ch == 0 and "x1_tm" in dbg_outs:
            dma("sync", dbg_outs["x1_tm"][:], x_tm[:], DX, waits=[(V_, x1v)])

    def phase_E(ch):
        par = ch % 2
        x_tm = x_tmd[:, par]
        x1_ready = st[ch]["x1_ready"]
        xqv = 0
        for i in range(NTOK):
            sl = rngB["i"] % 2
            rngB["i"] += 1
            rb = tmB[:, sl, :]
            s4 = ln_norm(x_tm[:, i, :], rb,
                         extra_waits=[(V_, x1_ready), rngB["g"][sl]])
            mx = stat[:, 15:16]
            sc4 = stat[:, 16:17]
            vr = sch.emit("vector", lambda e, rb=rb, mx=mx: e.reduce_max(
                out=mx, in_=rb, axis=AX.X, apply_absolute_value=True),
                waits=[(S_, s4)], inc=V_)
            sch.emit("vector", lambda e, v=vr: e.wait_ge(sch.sems["V"], v))
            vr = sch.emit("vector", lambda e, mx=mx, sc4=sc4: e.reciprocal(
                out=sc4, in_=mx), inc=V_)
            sch.emit("vector", lambda e, v=vr: e.wait_ge(sch.sems["V"], v))
            vv = sch.emit("vector", lambda e, sc4=sc4: e.tensor_scalar(
                out=sc4, in0=sc4, scalar1=127.0, scalar2=None, op0=ALU.mult),
                inc=V_)
            sch.emit("vector", lambda e, mx=mx, i=i: e.tensor_scalar(
                out=stat[:, 17 + i:18 + i], in0=mx, scalar1=1.0 / 127.0,
                scalar2=None, op0=ALU.mult), inc=None)
            s5 = sch.emit("scalar", lambda e, rb=rb, sc4=sc4: e.activation(
                out=f32r[0][:], in_=rb[:, 0:512], func=AF.Identity,
                scale=sc4, bias=magic_col[:, 0:1]), waits=[(V_, vv)], inc=S_)
            s6 = sch.emit("scalar", lambda e, rb=rb, sc4=sc4: e.activation(
                out=f32r[1][:], in_=rb[:, 512:1024], func=AF.Identity,
                scale=sc4, bias=magic_col[:, 0:1]), inc=S_)
            sch.emit("vector", lambda e, rb=rb: e.tensor_scalar(
                out=rb[:, 0:512], in0=f32r[0][:], scalar1=-MAGIC,
                scalar2=None, op0=ALU.add), waits=[(S_, s5)], inc=None)
            xqv = sch.emit("vector", lambda e, rb=rb: e.tensor_scalar(
                out=rb[:, 512:1024], in0=f32r[1][:], scalar1=-MAGIC,
                scalar2=None, op0=ALU.add), waits=[(S_, s6)], inc=V_)
            tp = sch.emit("tensor", lambda e, i=i: e.transpose(
                psTRf[0:1, 0:128], stat[:, 17 + i:18 + i], idf[:]),
                waits=[(V_, xqv), trg["trf"]], inc=P_)
            rv = sch.emit("vector", lambda e, i=i: e.tensor_copy(
                out=row_f[0:1, i * 128:(i + 1) * 128], in_=psTRf[0:1, 0:128]),
                waits=[(P_, tp)], inc=V_)
            trg["trf"] = (V_, rv)

            def dst_fn(jg, waits, i=i):
                return sch.emit("scalar", lambda e, i=i, jg=jg: e.activation(
                    out=fmS[:, jg * 4:(jg + 1) * 4, i * 128:(i + 1) * 128],
                    in_=psTR[:, jg * 512:(jg + 1) * 512].rearrange(
                        "p (a b) -> p a b", a=4), func=AF.Identity),
                    waits=waits, inc=S_)
            transpose_tile(lambda j, rb=rb: rb[:, j * 128:(j + 1) * 128],
                           dst_fn, (V_, xqv), ev_sem="S")
            rngB["g"][sl] = (P_, sch.val(P_))
        xhat_ready = sch.emit("vector", lambda e: e.nop(),
                              waits=[(S_, sch.val(S_))], inc=V_)
        invsx_row_ready = sch.val(V_)
        if ch == 0 and "xhat_fm" in dbg_outs:
            dma("sync", dbg_outs["xhat_fm"][:], fmS[:], DX,
                waits=[(V_, xhat_ready)])

        j, ps, g = rot.next()
        pm = sch.emit("tensor", lambda e, ps=ps: e.matmul(
            ps[:], ones_r_bf[:], row_f[0:1, 0:C], start=True, stop=True),
            waits=[(V_, invsx_row_ready)] + g, inc=P_)
        invsx_ready = sch.emit("vector", lambda e, ps=ps: e.tensor_copy(
            out=invsx_bc[:], in_=ps[:]), waits=[(P_, pm)], inc=V_)
        rot.set_guard(j, V_, invsx_ready)

        sch.emit("vector", lambda e: e.memset(mhi[:], -1e30), inc=None)
        mm0 = sch.emit("vector", lambda e: e.memset(mlo[:], 1e30), inc=V_)

        gu_state = {"sgs": ("S", 0)}
        for q in range(8):
            m0 = q * 512
            slot_g, wg = load_seg(s_fgT[:, m0:m0 + 512].rearrange(
                "(k p) m -> p k m", p=128))
            slot_u, wu = load_seg(s_fuT[:, m0:m0 + 512].rearrange(
                "(k p) m -> p k m", p=128))
            svg = seg_view(slot_g, 8, 512)
            svu = seg_view(slot_u, 8, 512)
            glast = ulast = None
            for mt in range(4):
                gm = q * 4 + mt
                jg_, psg, gg = rot.next()
                for k in range(8):
                    waits = ([(DW, wg), (V_, xhat_ready)] + gg) if k == 0 else ()
                    glast = sch.emit("tensor", lambda e, ps=psg, sv=svg, k=k,
                                     mt=mt: e.matmul(
                        ps[:], sv[:, k, mt * 128:(mt + 1) * 128], fmS[:, k, :],
                        start=(k == 0), stop=(k == 7)),
                        waits=waits, inc=(P_ if k == 7 else None))
                ju_, psu, gu_ = rot.next()
                for k in range(8):
                    waits = ([(DW, wu)] + gu_) if k == 0 else ()
                    ulast = sch.emit("tensor", lambda e, ps=psu, sv=svu, k=k,
                                     mt=mt: e.matmul(
                        ps[:], sv[:, k, mt * 128:(mt + 1) * 128], fmS[:, k, :],
                        start=(k == 0), stop=(k == 7)),
                        waits=waits, inc=(P_ if k == 7 else None))
                v1 = sch.emit("vector", lambda e, ps=psg: e.tensor_mul(
                    out=f32r[0][:], in0=ps[:], in1=invsx_bc[:]),
                    waits=[(P_, glast), (V_, invsx_ready), (V_, mm0),
                           gu_state["sgs"]], inc=V_)
                rot.set_guard(jg_, V_, v1)
                sgs = sch.emit("scalar", lambda e: e.activation(
                    out=sg_b[:], in_=f32r[0][:], func=AF.Silu,
                    scale=wgs_bc[:, 0:1]), waits=[(V_, v1)], inc=S_)
                gu_state["sgs"] = (S_, sgs)
                v2 = sch.emit("vector", lambda e, ps=psu: e.tensor_mul(
                    out=f32r[1][:], in0=ps[:], in1=invsx_bc[:]),
                    waits=[(P_, ulast)], inc=V_)
                rot.set_guard(ju_, V_, v2)
                sch.emit("vector", lambda e, gm=gm: e.tensor_mul(
                    out=gu_fm[:, gm, :], in0=f32r[1][:], in1=sg_b[:]),
                    waits=[(S_, sgs)], inc=None)
                sch.emit("vector", lambda e, gm=gm: e.tensor_max(
                    out=mhi[:], in0=mhi[:], in1=gu_fm[:, gm, :]), inc=None)
                sch.emit("vector", lambda e, gm=gm: e.tensor_tensor(
                    out=mlo[:], in0=mlo[:], in1=gu_fm[:, gm, :], op=ALU.min),
                    inc=(V_ if mt == 3 else None))
            seg_done(slot_g, P_, glast)
            seg_done(slot_u, P_, ulast)
        gu_ready = sch.val(V_)
        if ch == 0 and "gu_fm" in dbg_outs:
            dma("sync", dbg_outs["gu_fm"][:], gu_fm[:], DX,
                waits=[(V_, gu_ready)])

        sch.emit("vector", lambda e: e.tensor_scalar(
            out=mlo[:], in0=mlo[:], scalar1=-1.0, scalar2=None, op0=ALU.mult),
            inc=None)
        mab = sch.emit("vector", lambda e: e.tensor_max(
            out=mhi[:], in0=mhi[:], in1=mlo[:]), inc=V_)

        mtp = 0
        for i in range(NTOK):
            tp = sch.emit("tensor", lambda e, i=i: e.transpose(
                psTR[:, 0:128], mhi[:, i * 128:(i + 1) * 128], idbf[:]),
                waits=[(V_, mab), trg["tr"]], inc=P_)
            ev = sch.emit("vector", lambda e: e.tensor_copy(
                out=sg_b[:, 0:128], in_=psTR[:, 0:128]),
                waits=[(P_, tp), (P_, mtp)], inc=V_)
            trg["tr"] = (V_, ev)

            def vstep(fn, waits=()):
                v = sch.emit("vector", fn, waits=waits, inc=V_)
                sch.emit("vector", lambda e, v=v: e.wait_ge(sch.sems["V"], v))
                return v
            vstep(lambda e: e.reduce_max(
                out=stat[:, 21:22], in_=sg_b[:, 0:128], axis=AX.X))
            vstep(lambda e: e.reciprocal(
                out=stat[:, 22:23], in_=stat[:, 21:22]))
            vstep(lambda e: e.tensor_scalar(
                out=stat[:, 22:23], in0=stat[:, 22:23], scalar1=127.0,
                scalar2=None, op0=ALU.mult))
            vstep(lambda e: e.tensor_mul(
                out=stat[:, 21:22], in0=stat[:, 21:22], in1=wus_bc[:]))
            vstep(lambda e: e.tensor_mul(
                out=stat[:, 21:22], in0=stat[:, 21:22], in1=wds_bc[:]))
            cu = sch.emit("vector", lambda e: e.tensor_scalar(
                out=stat[:, 21:22], in0=stat[:, 21:22], scalar1=1.0 / 127.0,
                scalar2=None, op0=ALU.mult), inc=V_)
            for cidx, off in [(22, C), (21, 2 * C)]:
                tp = sch.emit("tensor", lambda e, cidx=cidx: e.transpose(
                    psTRf[0:1, 0:128], stat[:, cidx:cidx + 1], idf[:]),
                    waits=[(V_, cu), trg["trf"]], inc=P_)
                rv = sch.emit("vector", lambda e, i=i, off=off: e.tensor_copy(
                    out=row_f[0:1, off + i * 128:off + (i + 1) * 128],
                    in_=psTRf[0:1, 0:128]), waits=[(P_, tp)], inc=V_)
                trg["trf"] = (V_, rv)
            mtp = sch.val(P_)
        rows_ready = sch.val(V_)

        for dst, off in [(comb_bc, C), (unsc_bc, 2 * C)]:
            j, ps, g = rot.next()
            pm = sch.emit("tensor", lambda e, ps=ps, off=off: e.matmul(
                ps[:], ones_r_bf[:], row_f[0:1, off:off + C], start=True,
                stop=True), waits=[(V_, rows_ready)] + g, inc=P_)
            ev = sch.emit("vector", lambda e, ps=ps, dst=dst: e.tensor_copy(
                out=dst[:], in_=ps[:]), waits=[(P_, pm)], inc=V_)
            rot.set_guard(j, V_, ev)
        comb_ready = sch.val(V_)

        f32g = [("S", 0), ("S", 0)]
        for gm in range(32):
            v1 = sch.emit("vector", lambda e, gm=gm: e.tensor_mul(
                out=f32r[gm % 2][:], in0=gu_fm[:, gm, :], in1=comb_bc[:]),
                waits=([(V_, comb_ready)] if gm < 2 else []) + [f32g[gm % 2]],
                inc=V_)
            sq = sch.emit("scalar", lambda e, gm=gm: e.activation(
                out=f32r[gm % 2][:], in_=f32r[gm % 2][:], func=AF.Identity,
                bias=magic_col[:, 0:1]), waits=[(V_, v1)], inc=S_)
            sch.emit("scalar", lambda e, v=sq: e.wait_ge(sch.sems["S"], v))
            sb2 = sch.emit("scalar", lambda e, gm=gm: e.activation(
                out=gu_fm[:, gm, :], in_=f32r[gm % 2][:], func=AF.Identity,
                bias=negmagic_col[:, 0:1]), inc=S_)
            f32g[gm % 2] = (S_, sb2)
        x2h_s = sch.val(S_)
        x2h_ready = sch.emit("vector", lambda e: e.nop(),
                             waits=[(S_, x2h_s)], inc=V_)
        if ch == 0 and "x2hat_fm" in dbg_outs:
            dma("sync", dbg_outs["x2hat_fm"][:], gu_fm[:], DX,
                waits=[(V_, x2h_ready)])

        def down_evac(mt, ps, waits):
            v = sch.emit("vector", lambda e, mt=mt, ps=ps: e.tensor_mul(
                out=f_fm[:, mt, :], in0=ps[:], in1=unsc_bc[:]),
                waits=waits, inc=V_)
            return V_, v
        fm_mm(s_fdT, INNER, D, gu_fm, down_evac, x2h_ready, seg_mt=1,
              extra_mm_waits=[(V_, comb_ready)])
        f_ready = sch.val(V_)
        gd["zdt"] = [(P_, sch.val(P_))]
        if ch == 0 and "f_fm" in dbg_outs:
            dma("sync", dbg_outs["f_fm"][:], f_fm[:], DX, waits=[(V_, f_ready)])

        t0 = ch * C
        xv = 0
        for i in range(NTOK):
            sl = rngB["i"] % 2
            rngB["i"] += 1
            rb = tmB[:, sl, :]

            def dst_fn(jg, waits, i=i, rb=rb):
                return sch.emit("vector", lambda e, jg=jg, rb=rb: e.tensor_copy(
                    out=rb[:, jg * 512:(jg + 1) * 512],
                    in_=psTR[:, jg * 512:(jg + 1) * 512]),
                    waits=waits + ([rngB["g"][sl]] if jg == 0 else []), inc=V_)
            transpose_tile(lambda j, i=i: f_fm[:, j, i * 128:(i + 1) * 128],
                           dst_fn, (V_, f_ready))
            sl2 = rngT["i"] % 2
            rngT["i"] += 1
            ra = tmA[:, sl2, :]
            s7 = ln_norm(rb, ra, extra_waits=[(V_, sch.val(V_)),
                                              rngT["g"][sl2]])
            rngB["g"][sl] = (S_, s7)
            xv = sch.emit("vector", lambda e, i=i, ra=ra: e.tensor_add(
                out=x_tm[:, i, :], in0=x_tm[:, i, :], in1=ra),
                waits=[(S_, s7)], inc=V_)
            rngT["g"][sl2] = (V_, xv)
            gd["xstore"][par] = dma(
                "scalar", d_out[t0 + i * 128:t0 + (i + 1) * 128, :],
                x_tm[:, i, :], DX, waits=[(V_, xv)])
        gd["xpar"][par] = [(V_, xv)]

    nch_run = NCH if _STAGE >= 7 else (0 if _STAGE < 2 else 1)
    for ch in range(nch_run):
        phase_A(ch)
        if ch > 0 and _STAGE >= 5:
            phase_D(ch - 1)
            if _STAGE >= 6:
                phase_E(ch - 1)
        if _STAGE >= 3:
            phase_B(ch)
        if _STAGE >= 4:
            phase_scan(ch)
        if ch == 0 and "xn0_tm" in dbg_outs:
            dma("sync", dbg_outs["xn0_tm"][:], xn0_tmd[:, 0], DX,
                waits=[(V_, st[0]["xn_ready"]), (DW, sch.val(DW))])
    if nch_run and _STAGE >= 5:
        phase_D(nch_run - 1)
        if _STAGE >= 6:
            phase_E(nch_run - 1)

    sch.emit("sync", lambda e: None, waits=[(DX, sch.val(DX)),
                                            (DW, sch.val(DW))])
    sch.realize(block)
    es.close()
    return nc, sorted(dbg_outs.keys())


def kernel(L0_attn_x, L0_attn_norm_w, L0_attn_norm_b, L0_attn_W_in,
           L0_attn_W_gate, L0_attn_W_dt, L0_attn_b_dt, L0_attn_W_B,
           L0_attn_W_C, L0_attn_A_log, L0_attn_W_out, L0_ffn_w_gate,
           L0_ffn_w_up, L0_ffn_w_down):
    nc, dbg_names = build_program()

    sel = np.zeros((32, 32 * 128), dtype=BFNP)
    for s in range(32):
        sel[s, s * 128:(s + 1) * 128] = 1

    common = {
        "norm_w": np.ascontiguousarray(L0_attn_norm_w),
        "norm_b": np.ascontiguousarray(L0_attn_norm_b),
        "W_in": np.ascontiguousarray(L0_attn_W_in),
        "W_gate": np.ascontiguousarray(L0_attn_W_gate),
        "W_dt": np.ascontiguousarray(L0_attn_W_dt),
        "b_dt": np.ascontiguousarray(L0_attn_b_dt),
        "W_B": np.ascontiguousarray(L0_attn_W_B),
        "W_C": np.ascontiguousarray(L0_attn_W_C),
        "A_log": np.ascontiguousarray(L0_attn_A_log),
        "W_out": np.ascontiguousarray(L0_attn_W_out),
        "ffn_w_gate": np.ascontiguousarray(L0_ffn_w_gate),
        "ffn_w_up": np.ascontiguousarray(L0_ffn_w_up),
        "ffn_w_down": np.ascontiguousarray(L0_ffn_w_down),
        "ident_bf": np.eye(128, dtype=BFNP),
        "ident_f32": np.eye(128, dtype=np.float32),
        "ones_r_bf": np.ones((1, 128), dtype=BFNP),
        "ones_r_f": np.ones((1, 128), dtype=np.float32),
        "ones_c_f": np.ones((128, 1), dtype=np.float32),
        "sel32": sel,
    }
    x = np.ascontiguousarray(L0_attn_x)
    in_maps = [dict(common, x=np.ascontiguousarray(x[b])) for b in range(8)]
    res = run_bass_kernel_spmd(nc, in_maps, core_ids=list(range(8)))
    out = np.stack([res.results[b]["out"] for b in range(8)], axis=0)
    kernel._results = res.results
    return out


# version 13
# speedup vs baseline: 1.0032x; 1.0032x over previous
import numpy as np
import ml_dtypes
from contextlib import ExitStack

import concourse.bass as bass
import concourse.mybir as mybir
from concourse.bass_utils import run_bass_kernel_spmd

F32 = mybir.dt.float32
BF16 = mybir.dt.bfloat16
AF = mybir.ActivationFunctionType
ALU = mybir.AluOpType
AX = mybir.AxisListType
BFNP = ml_dtypes.bfloat16

T = 4096
D = 1024
DI = 2048
DS = 16
INNER = 4096
C = 512
NCH = T // C
NTOK = C // 128
MAGIC = 12582912.0
LN_EPS = 1e-5

ENGINES = ("tensor", "vector", "scalar", "gpsimd", "sync")

_DBG = set()
_STAGE = 99
_PREP = 9


class Sched:
    def __init__(self, nc, es):
        self.nc = nc
        self.q = {e: [] for e in ENGINES}
        self.cnt = {}
        self.sems = {}
        for name in ("P", "V", "S", "G", "DW", "DX"):
            self.sems[name] = es.enter_context(nc.semaphore(name))
            self.cnt[name] = 0

    def emit(self, eng, fn, waits=(), inc=None, inc_n=1):
        w = tuple((s, v) for (s, v) in waits if v > 0)
        if inc is not None:
            self.cnt[inc] += inc_n
        self.q[eng].append((fn, w, inc, inc_n))
        return self.cnt[inc] if inc is not None else None

    def val(self, s):
        return self.cnt[s]

    def realize(self, block):
        sems = self.sems
        for eng in ENGINES:
            ops = self.q[eng]

            def make(ops):
                def body(e):
                    last = {}
                    for fn, waits, inc, inc_n in ops:
                        for s, v in waits:
                            if v > last.get(s, 0):
                                e.wait_ge(sems[s], v)
                                last[s] = v
                        inst = fn(e)
                        if inc is not None:
                            if inst is None:
                                inst = e.nop()
                            inst.then_inc(sems[inc], inc_n)
                return body

            getattr(block, eng)(make(ops))


def build_program():
    nc = bass.Bass("TRN2", target_bir_lowering=False)
    es = ExitStack()

    dx = nc.dram_tensor("x", [T, D], F32, kind="ExternalInput").ap()
    d_nw = nc.dram_tensor("norm_w", [D], F32, kind="ExternalInput").ap()
    d_nb = nc.dram_tensor("norm_b", [D], F32, kind="ExternalInput").ap()
    d_win = nc.dram_tensor("W_in", [D, DI], F32, kind="ExternalInput").ap()
    d_wgate = nc.dram_tensor("W_gate", [D, DI], F32, kind="ExternalInput").ap()
    d_wdt = nc.dram_tensor("W_dt", [DI, DI], F32, kind="ExternalInput").ap()
    d_bdt = nc.dram_tensor("b_dt", [DI], F32, kind="ExternalInput").ap()
    d_wB = nc.dram_tensor("W_B", [DI, DS], F32, kind="ExternalInput").ap()
    d_wC = nc.dram_tensor("W_C", [DI, DS], F32, kind="ExternalInput").ap()
    d_alog = nc.dram_tensor("A_log", [DI, DS], F32, kind="ExternalInput").ap()
    d_wout = nc.dram_tensor("W_out", [DI, D], F32, kind="ExternalInput").ap()
    d_fg = nc.dram_tensor("ffn_w_gate", [INNER, D], F32, kind="ExternalInput").ap()
    d_fu = nc.dram_tensor("ffn_w_up", [INNER, D], F32, kind="ExternalInput").ap()
    d_fd = nc.dram_tensor("ffn_w_down", [D, INNER], F32, kind="ExternalInput").ap()
    d_idbf = nc.dram_tensor("ident_bf", [128, 128], BF16, kind="ExternalInput").ap()
    d_idf = nc.dram_tensor("ident_f32", [128, 128], F32, kind="ExternalInput").ap()
    d_ones_r_bf = nc.dram_tensor("ones_r_bf", [1, 128], BF16, kind="ExternalInput").ap()
    d_ones_r_f = nc.dram_tensor("ones_r_f", [1, 128], F32, kind="ExternalInput").ap()
    d_ones_c_f = nc.dram_tensor("ones_c_f", [128, 1], F32, kind="ExternalInput").ap()
    d_sel = nc.dram_tensor("sel32", [32, 32 * 128], BF16, kind="ExternalInput").ap()

    d_out = nc.dram_tensor("out", [T, D], F32, kind="ExternalOutput").ap()

    s_win = nc.dram_tensor("s_win", [D, DI], BF16, kind="Internal").ap()
    s_wgate = nc.dram_tensor("s_wgate", [D, DI], BF16, kind="Internal").ap()
    s_wdt = nc.dram_tensor("s_wdt", [DI, DI], BF16, kind="Internal").ap()
    s_wout = nc.dram_tensor("s_wout", [DI, D], BF16, kind="Internal").ap()
    s_fgT = nc.dram_tensor("s_fgT", [D, INNER], BF16, kind="Internal").ap()
    s_fuT = nc.dram_tensor("s_fuT", [D, INNER], BF16, kind="Internal").ap()
    s_fdT = nc.dram_tensor("s_fdT", [INNER, D], BF16, kind="Internal").ap()

    dbg_outs = {}
    for name, shape, dt in [
        ("xn0_tm", [128, NTOK, D], BF16), ("xn_fm", [128, 8, C], BF16),
        ("z_fm", [128, 16, C], BF16), ("dt_fm", [128, 16, C], BF16),
        ("gate_fm", [128, 16, C], BF16), ("bc_sb", [32, C], BF16),
        ("y_fm", [128, 16, C], BF16), ("attn_fm", [128, 8, C], BF16),
        ("x1_tm", [128, NTOK, D], F32), ("xhat_fm", [128, 8, C], BF16),
        ("gu_fm", [128, 32, C], BF16), ("x2hat_fm", [128, 32, C], BF16),
        ("wq_probe", [128, 512], BF16), ("a_probe", [128, 16, 16], F32),
        ("tmA", [128, NTOK, D], BF16), ("wq2_probe", [128, 512], BF16),
        ("f_fm", [128, 8, C], BF16),
        ("scales_probe", [128, 8], F32),
    ]:
        if name in _DBG:
            dbg_outs[name] = nc.dram_tensor("dbg_" + name, shape, dt,
                                            kind="ExternalOutput").ap()

    sch = Sched(nc, es)
    P_, V_, S_, G_, DW, DX = "P", "V", "S", "G", "DW", "DX"

    import numpy as _np
    _alloc_log = []

    def SBe(e_, n, s, d):
        bpp = int(_np.prod(s[1:])) * mybir.dt.size(d)
        _alloc_log.append((n, bpp))
        return e_.enter_context(nc.sbuf_tensor(n, s, d))
    SB = lambda n, s, d: SBe(es, n, s, d)
    build_program._alloc_log = _alloc_log
    PSUM = lambda n, s, d: es.enter_context(nc.psum_tensor(n, s, d))

    idbf = SB("sb_idbf", [128, 128], BF16)
    idf = SB("sb_idf", [128, 128], F32)
    ones_r_bf = SB("sb_ones_r_bf", [1, 128], BF16)
    ones_r_f = SB("sb_ones_r_f", [1, 128], F32)
    ones_c_f = SB("sb_ones_c_f", [128, 1], F32)
    sel = SB("sb_sel", [32, 32 * 128], BF16)
    nw_rep = SB("nw_rep", [128, D], BF16)
    nb_rep = SB("nb_rep", [128, D], BF16)
    a_sb = SB("a_sb", [128, 16, 16], F32)
    bdt_sb = SB("bdt_sb", [128, 16], F32)
    wbc = SB("wbc", [128, 16, 32], BF16)
    wgs_bc = SB("wgs_bc", [128, 1], F32)
    wus_bc = SB("wus_bc", [128, 1], F32)
    wds_bc = SB("wds_bc", [128, 1], F32)
    wgr_bc = SB("wgr_bc", [128, 1], F32)
    wur_bc = SB("wur_bc", [128, 1], F32)
    wdr_bc = SB("wdr_bc", [128, 1], F32)
    H = SB("H", [128, 256], F32)
    magic_col = SB("magic_col", [128, 1], F32)
    eps_col = SB("eps_col", [128, 1], F32)
    one_col = SB("one_col", [128, 1], F32)
    neghalf_col = SB("neghalf_col", [128, 1], F32)
    negmagic_col = SB("negmagic_col", [128, 1], F32)

    NSLOT = 2
    SEGSZ = 4096
    wslot = [SB(f"wslot{j}", [128, SEGSZ], BF16) for j in range(NSLOT)]

    psROT = [PSUM(f"ps{j}", [128, 512], F32) for j in range(5)]
    psTR = PSUM("psTR", [128, 1024], BF16)
    psTRf = PSUM("psTRf", [128, 512], F32)

    block = es.enter_context(nc.Block())

    class Rot:
        def __init__(self, banks):
            self.banks = banks
            self.i = 0
            self.guard = [("V", 0)] * len(banks)

        def next(self):
            j = self.i % len(self.banks)
            self.i += 1
            return j, self.banks[j], [self.guard[j]]

        def set_guard(self, j, sem, val):
            self.guard[j] = (sem, val)

    rot = Rot(psROT)
    trg = {"tr": ("V", 0), "trf": ("V", 0)}

    def pace(eng="tensor"):
        sch.emit(eng, lambda e: e.wait_ge(sch.sems["DW"], 16))

    def dma(queue, out_ap, in_ap, sem, waits=()):
        sch.emit(queue, lambda e, o=out_ap, i=in_ap: e.dma_start(out=o, in_=i),
                 waits=waits, inc=sem, inc_n=16)
        return sch.val(sem)

    for dst, src in [(idbf, d_idbf), (idf, d_idf), (ones_r_bf, d_ones_r_bf),
                     (ones_r_f, d_ones_r_f), (ones_c_f, d_ones_c_f),
                     (sel, d_sel)]:
        dma("sync", dst[:], src[:], DW)
    consts_dw = dma("sync", a_sb[:], d_alog.rearrange("(dc p) s -> p dc s", p=128), DW)

    s0 = sch.emit("scalar", lambda e: e.activation(out=a_sb[:], in_=a_sb[:],
                  func=AF.Exp), waits=[(DW, consts_dw)], inc=S_)
    va = sch.emit("vector", lambda e: e.tensor_scalar(out=a_sb[:], in0=a_sb[:],
             scalar1=1e-6, scalar2=1.0, op0=ALU.max, op1=ALU.min),
             waits=[(S_, s0)], inc=V_)
    sch.emit("vector", lambda e, v=va: e.wait_ge(sch.sems["V"], v))
    a_ready = sch.emit("vector", lambda e: e.tensor_scalar(out=a_sb[:],
                       in0=a_sb[:], scalar1=-1.0, scalar2=None, op0=ALU.mult),
                       inc=V_)
    sch.emit("vector", lambda e: e.memset(H[:], 0.0), inc=V_)
    bdt_row = SB("bdt_row", [16, 128], F32)
    bdtw = dma("sync", bdt_row[:], d_bdt.rearrange("(k p) -> k p", p=128), DW)
    bdt_tp = sch.emit("tensor", lambda e: e.transpose(
        psTRf[:, 0:16], bdt_row[:], idf[0:16, 0:16]), waits=[(DW, bdtw)], inc=P_)
    bdt_done = sch.emit("vector", lambda e: e.tensor_copy(
        out=bdt_sb[:], in_=psTRf[:, 0:16]), waits=[(P_, bdt_tp)], inc=V_)
    trg["trf"] = ("V", bdt_done)
    sch.emit("vector", lambda e: e.memset(magic_col[:], MAGIC), inc=None)
    sch.emit("vector", lambda e: e.memset(eps_col[:], LN_EPS), inc=None)
    sch.emit("vector", lambda e: e.memset(one_col[:], 1.0), inc=None)
    sch.emit("vector", lambda e: e.memset(neghalf_col[:], -0.5), inc=None)
    sch.emit("vector", lambda e: e.memset(negmagic_col[:], -MAGIC), inc=V_)

    es_prep = ExitStack()
    nrow_f = SBe(es_prep, "nrow_f", [1, D], F32)
    nrow_b = SBe(es_prep, "nrow_b", [1, D], BF16)
    wbc_f = SBe(es_prep, "wbc_f", [128, 16, 16], F32)
    wtmp_f = [SBe(es_prep, f"wtmp_f{j}", [128, 4096], F32) for j in range(2)]
    wtmp_b = [SBe(es_prep, f"wtmp_b{j}", [128, 4096], BF16) for j in range(2)]
    sabs = SBe(es_prep, "sabs", [128, 2], F32)
    pr = SBe(es_prep, "pr", [128, 2, 512], BF16)
    sc8 = SBe(es_prep, "sc8", [1, 8], F32)

    rep_pm = 0
    for vec_d, rep in ([(d_nw, nw_rep), (d_nb, nb_rep)] if _PREP >= 1 else []):
        w = dma("sync", nrow_f[:], vec_d[None, :], DW,
                waits=[(V_, sch.val(V_))])
        cv = sch.emit("vector", lambda e: e.tensor_copy(out=nrow_b[:], in_=nrow_f[:]),
                      waits=[(DW, w), (P_, rep_pm)], inc=V_)
        for ns in range(2):
            j, ps, g = rot.next()
            rep_pm = sch.emit("tensor", lambda e, ps=ps, ns=ns: e.matmul(
                ps[:], ones_r_bf[:], nrow_b[0:1, ns * 512:(ns + 1) * 512],
                start=True, stop=True), waits=[(V_, cv)] + g, inc=P_)
            ev = sch.emit("vector", lambda e, ps=ps, rep=rep, ns=ns: e.tensor_copy(
                out=rep[:, ns * 512:(ns + 1) * 512], in_=ps[:]),
                waits=[(P_, rep_pm)], inc=V_)
            rot.set_guard(j, V_, ev)

    for col0, src in ([(0, d_wB), (16, d_wC)] if _PREP >= 1 else []):
        w = dma("sync", wbc_f[:], src.rearrange("(k p) s -> p k s", p=128), DW,
                waits=[(V_, sch.val(V_))])
        sch.emit("vector", lambda e, c0=col0: e.tensor_copy(
            out=wbc[:, :, c0:c0 + 16], in_=wbc_f[:]), waits=[(DW, w)], inc=V_)
    wbc_ready = sch.val(V_)

    prep_cast_guard = [("V", 0), ("V", 0)]

    def cast_weight(src, dst, rows, cols):
        nb = min(4096 // cols, rows // 128)
        segrows = nb * 128
        for sgi in range(rows // segrows):
            r0 = sgi * segrows
            h = sgi % 2
            wtf = wtmp_f[h][:, 0:nb * cols].rearrange("p (a b) -> p a b", a=nb)
            wtb = wtmp_b[h][:, 0:nb * cols].rearrange("p (a b) -> p a b", a=nb)
            w = dma("sync", wtf,
                    src[r0:r0 + segrows, :].rearrange("(k p) m -> p k m", p=128),
                    DW, waits=[prep_cast_guard[h]])
            cv = sch.emit("vector", lambda e, wtf=wtf, wtb=wtb: e.tensor_copy(
                out=wtb, in_=wtf), waits=[(DW, w)], inc=V_)
            prep_cast_guard[h] = (V_, cv)
            dma("sync",
                dst[r0:r0 + segrows, :].rearrange("(k p) m -> p k m", p=128),
                wtb, DW, waits=[(V_, cv)])

    if _PREP >= 2:
        cast_weight(d_win, s_win, D, DI)
        cast_weight(d_wgate, s_wgate, D, DI)
        cast_weight(d_wdt, s_wdt, DI, DI)
        cast_weight(d_wout, s_wout, DI, D)

    qt_state = {"pr_guard": [("V", 0), ("V", 0)], "prslot": [0],
                "tpg": [("P", 0), ("P", 0)]}

    def quant_transpose(src, dstT, rows, cols, s_bc, r_bc, si, p2=3):
        last_tp = sch.val(P_)
        nb = min(4096 // cols, rows // 128)
        segrows = nb * 128
        for sgi in range(rows // segrows):
            r0 = sgi * segrows
            h = sgi % 2
            wtf = wtmp_f[h][:, 0:nb * cols].rearrange("p (a b) -> p a b", a=nb)
            w = dma("sync", wtf,
                    src[r0:r0 + segrows, :].rearrange("(k p) m -> p k m", p=128),
                    DW, waits=[prep_cast_guard[h]])
            if sgi == 0:
                vr = sch.emit("vector", lambda e, wtf=wtf: e.reduce_sum(
                    out=sabs[:, 0:1], in_=wtf, axis=AX.XY,
                    apply_absolute_value=True), waits=[(DW, w)], inc=V_)
                prep_cast_guard[h] = (V_, vr)
            else:
                vr = sch.emit("vector", lambda e, wtf=wtf: e.reduce_sum(
                    out=sabs[:, 1:2], in_=wtf, axis=AX.XY,
                    apply_absolute_value=True), waits=[(DW, w)], inc=V_)
                prep_cast_guard[h] = (V_, vr)
                sch.emit("vector", lambda e, v=vr: e.wait_ge(sch.sems["V"], v))
                sch.emit("vector", lambda e: e.tensor_add(
                    out=sabs[:, 0:1], in0=sabs[:, 0:1], in1=sabs[:, 1:2]), inc=V_)
        pm = sch.emit("tensor", lambda e: e.transpose(
            psTRf[0:1, 0:128], sabs[:, 0:1], idf[:]),
            waits=[(V_, sch.val(V_)), trg["trf"]], inc=P_)
        sl = sc8[0:1, si:si + 1]
        sl2 = sc8[0:1, si + 1:si + 2]
        vr = sch.emit("vector", lambda e: e.reduce_sum(
            out=sabs[0:1, 1:2], in_=psTRf[0:1, 0:128], axis=AX.X),
            waits=[(P_, pm)], inc=V_)
        sch.emit("vector", lambda e, v=vr: e.wait_ge(sch.sems["V"], v))
        ev = sch.emit("vector", lambda e, sl=sl: e.tensor_scalar(
            out=sl, in0=sabs[0:1, 1:2], scalar1=1.0 / (rows * cols),
            scalar2=1e-5, op0=ALU.mult, op1=ALU.max), inc=V_)
        trg["trf"] = (V_, ev)
        sch.emit("vector", lambda e, v=ev: e.wait_ge(sch.sems["V"], v))
        sch.emit("vector", lambda e, sl=sl, sl2=sl2: e.reciprocal(
            out=sl2, in_=sl), inc=V_)
        vv = sch.val(V_)
        sch.emit("vector", lambda e, v=vv: e.wait_ge(sch.sems["V"], v))
        for val_ap, bc in [(sl, s_bc), (sl2, r_bc)]:
            j, ps, g = rot.next()
            pm = sch.emit("tensor", lambda e, ps=ps, v=val_ap: e.matmul(
                ps[:, 0:1], ones_r_f[:], v, start=True, stop=True),
                waits=[(V_, vv)] + g, inc=P_)
            ev = sch.emit("vector", lambda e, ps=ps, bc=bc: e.tensor_copy(
                out=bc[:], in_=ps[:, 0:1]), waits=[(P_, pm)], inc=V_)
            rot.set_guard(j, V_, ev)
        scales_v = sch.val(V_)
        if p2 == 0:
            return
        pr_guard = qt_state["pr_guard"]
        prslot = qt_state["prslot"]
        for sgi in range(rows // segrows):
            r0 = sgi * segrows
            h = sgi % 2
            wtf = wtmp_f[h][:, 0:nb * cols].rearrange("p (a b) -> p a b", a=nb)
            wtb = wtmp_b[h][:, 0:nb * cols].rearrange("p (a b) -> p a b", a=nb)
            w = dma("sync", wtf,
                    src[r0:r0 + segrows, :].rearrange("(k p) m -> p k m", p=128),
                    DW, waits=[prep_cast_guard[h]])
            sq = sch.emit("scalar", lambda e, wtf=wtf: e.activation(
                out=wtf, in_=wtf, func=AF.Identity,
                scale=r_bc[:, 0:1], bias=magic_col[:, 0:1]),
                waits=[(DW, w), (V_, scales_v)], inc=S_)
            sch.emit("vector", lambda e, wtf=wtf: e.tensor_scalar(
                out=wtf, in0=wtf, scalar1=-MAGIC,
                scalar2=-1.0, op0=ALU.add, op1=ALU.max),
                waits=[(S_, sq), qt_state["tpg"][h]], inc=None)
            cv = sch.emit("vector", lambda e, wtf=wtf, wtb=wtb: e.tensor_scalar(
                out=wtb, in0=wtf, scalar1=1.0,
                scalar2=None, op0=ALU.min), inc=V_)
            prep_cast_guard[h] = (V_, cv)
            if p2 == 1:
                continue
            for jb in range(cols // 128):
                for ibg in range((nb + 3) // 4):
                    nbl = min(4, nb - ibg * 4)
                    ltp = None
                    for ib2 in range(nbl):
                        ib = ibg * 4 + ib2
                        w = ([(V_, cv), trg["tr"]] if ib2 == 0
                             else [(P_, ltp)])
                        ltp = sch.emit("tensor", lambda e, ib=ib, ib2=ib2,
                                       jb=jb, wtb=wtb: e.transpose(
                            psTR[:, ib2 * 128:(ib2 + 1) * 128],
                            wtb[:, ib, jb * 128:(jb + 1) * 128], idbf[:]),
                            waits=w, inc=P_)
                    last_tp = ltp
                    qt_state["tpg"][h] = (P_, ltp)
                    slot = prslot[0] % 2
                    prslot[0] += 1
                    ev = sch.emit("vector", lambda e, slot=slot, nbl=nbl:
                                  e.tensor_copy(
                        out=pr[:, slot, 0:nbl * 128], in_=psTR[:, 0:nbl * 128]),
                        waits=[(P_, last_tp), pr_guard[slot]], inc=V_)
                    trg["tr"] = (V_, ev)
                    if p2 == 2:
                        pr_guard[slot] = (V_, ev)
                        continue
                    c0 = r0 + ibg * 512
                    st = dma("sync",
                             dstT[jb * 128:(jb + 1) * 128, c0:c0 + nbl * 128],
                             pr[:, slot, 0:nbl * 128], DW, waits=[(V_, ev)])
                    pr_guard[slot] = (DW, st)

    if _PREP >= 3:
        quant_transpose(d_fg, s_fgT, INNER, D, wgs_bc, wgr_bc, 0,
                        p2={3: 0, 4: 1, 5: 2, 6: 3}.get(_PREP, 3))
    if _PREP >= 7:
        quant_transpose(d_fu, s_fuT, INNER, D, wus_bc, wur_bc, 2)
        quant_transpose(d_fd, s_fdT, D, INNER, wds_bc, wdr_bc, 4)

    if "wq_probe" in dbg_outs and _PREP >= 6:
        w = dma("sync", pr[:, 0, :], s_fgT[0:128, 0:512], DW,
                waits=[(DW, sch.val(DW))])
        dma("sync", dbg_outs["wq_probe"][:], pr[:, 0, :], DX, waits=[(DW, w)])
    if "wq2_probe" in dbg_outs and _PREP >= 7:
        w = dma("sync", pr[:, 1, :], s_fdT[0:128, 0:512], DW,
                waits=[(DW, sch.val(DW))])
        dma("sync", dbg_outs["wq2_probe"][:], pr[:, 1, :], DX, waits=[(DW, w)])
    if "a_probe" in dbg_outs:
        dma("sync", dbg_outs["a_probe"][:], a_sb[:], DX, waits=[(V_, a_ready)])
    if "scales_probe" in dbg_outs and _PREP >= 3:
        pass
    if "scales_probe" in dbg_outs and _PREP >= 3:
        sp_b = SBe(es_prep, "sp_b", [128, 8], F32)
        for jj, bc in enumerate([wgs_bc, wgr_bc, wus_bc, wur_bc, wds_bc, wdr_bc]):
            sch.emit("vector", lambda e, jj=jj, bc=bc: e.tensor_copy(
                out=sp_b[:, jj:jj + 1], in_=bc[:]), inc=V_)
        dma("sync", dbg_outs["scales_probe"][:], sp_b[:], DX,
            waits=[(V_, sch.val(V_))])

    es_prep.close()
    prep_v = sch.val(V_)
    prep_s = sch.val(S_)
    prep_p = sch.val(P_)
    prep_dw = sch.val(DW)

    x_tmd = SB("x_tm", [128, 2, NTOK, D], F32)
    xn0_tmd = SB("xn0_tm", [128, 2, NTOK, D], BF16)
    tmA = SB("tmA", [128, 2, D], BF16)
    tmB = SB("tmB", [128, 2, D], BF16)
    xn_fm = SB("xn_fm", [128, 8, C], BF16)
    fmS = SB("fmS", [128, 8, C], BF16)
    gu_fm = SB("gu_fm", [128, 32, C], BF16)
    z_fm = gu_fm[:, 0:16, :]
    dt_fm = gu_fm[:, 16:32, :]
    gate_fm = SB("gate_fm", [128, 16, C], BF16)
    y_fm = SB("y_fm", [128, 16, C], BF16)
    f_fm = SB("f_fm", [128, 8, C], BF16)
    bc_sb = SB("bc_sb", [32, C], BF16)
    b_bc = SB("b_bc", [128, C], BF16)
    c_bc = SB("c_bc", [128, C], BF16)
    dAr = SB("dAr", [128, 2, C], BF16)
    br_s = SB("br", [128, C], BF16)
    hr = SB("hr", [128, 2, C], BF16)
    cmr2 = SB("cmr2", [128, 2, C], BF16)
    sp_r = SB("sp_r", [128, 1, C], BF16)
    dump = SB("dump", [128, D], BF16)
    f32r = [SB(f"f32r{j}", [128, C], F32) for j in range(2)]
    mhi = SB("mhi", [128, C], BF16)
    mlo = SB("mlo", [128, C], BF16)
    invsx_bc = SB("invsx_bc", [128, C], F32)
    comb_bc = SB("comb_bc", [128, C], F32)
    unsc_bc = SB("unsc_bc", [128, C], F32)
    row_f = SB("row_f", [1, 3 * C], BF16)
    stat = SB("stat", [128, 48], F32)
    statb = SB("statb", [128, 4], F32)
    sg_b = SB("sg_b", [128, C], BF16)

    seg_state = {"n": 0, "free": [("P", 0)] * NSLOT}

    def load_seg(dram_view):
        n = seg_state["n"]
        slot = n % NSLOT
        a = dram_view.shape[1]
        b = dram_view.shape[2]
        dst = wslot[slot][:, 0:a * b].rearrange("p (a b) -> p a b", a=a)
        w = dma("sync", dst, dram_view, DW,
                waits=[seg_state["free"][slot], (DW, prep_dw)])
        seg_state["n"] = n + 1
        return slot, w

    def seg_view(slot, a, b):
        return wslot[slot][:, 0:a * b].rearrange("p (a b) -> p a b", a=a)

    def seg_done(slot, sem, val):
        seg_state["free"][slot] = (sem, val)

    ln_pool = {"i": 0, "guard": [("V", 0)] * 4}

    def ln_norm(src_ap, out_ap, extra_waits=()):
        slot = ln_pool["i"] % 4
        ln_pool["i"] += 1
        c = 3 * slot
        bncol = stat[:, 24 + 12 * (slot % 2):24 + 12 * (slot % 2) + 12]
        mv = stat[:, c:c + 2]
        c2 = stat[:, c + 2:c + 3]
        bias = statb[:, slot:slot + 1]
        ew = list(extra_waits) + [ln_pool["guard"][slot]]
        sch.emit("vector", lambda e: e.bn_stats(
            out=bncol[:, 0:6], in_=src_ap[:, 0:512]), waits=ew, inc=V_)
        v1 = sch.emit("vector", lambda e: e.bn_stats(
            out=bncol[:, 6:12], in_=src_ap[:, 512:1024]), inc=V_)
        sch.emit("vector", lambda e: e.wait_ge(sch.sems["V"], v1))
        v2 = sch.emit("vector", lambda e: e.bn_aggr(
            out=mv, in_=bncol), inc=V_)
        sch.emit("vector", lambda e: e.wait_ge(sch.sems["V"], v2))
        v3 = sch.emit("vector", lambda e: e.tensor_scalar(
            out=mv[:, 0:1], in0=mv[:, 0:1], scalar1=-1.0, scalar2=None,
            op0=ALU.mult), inc=V_)
        s1 = sch.emit("scalar", lambda e: e.activation(
            out=c2, in_=mv[:, 1:2], func=AF.Ln, bias=eps_col[:, 0:1]),
            waits=[(V_, v2)], inc=S_)
        sch.emit("scalar", lambda e: e.wait_ge(sch.sems["S"], s1))
        s2 = sch.emit("scalar", lambda e: e.activation(
            out=c2, in_=c2, func=AF.Exp, scale=neghalf_col[:, 0:1]), inc=S_)
        sch.emit("scalar", lambda e: e.wait_ge(sch.sems["S"], s2))
        s3 = sch.emit("scalar", lambda e: e.activation(
            out=bias, in_=mv[:, 0:1], func=AF.Identity, scale=c2),
            waits=[(V_, v3)], inc=S_)
        sch.emit("scalar", lambda e: e.wait_ge(sch.sems["S"], s3))
        ap_s = sch.emit("scalar", lambda e: e.activation(
            out=out_ap, in_=src_ap, func=AF.Identity, scale=c2, bias=bias),
            inc=S_)
        ln_pool["guard"][slot] = (S_, ap_s)
        return ap_s

    def fm_mm(wsrc, rows, out_cols, rhs_fm, evac, rhs_ready, seg_mt=4,
              extra_mm_waits=()):
        K = rows // 128
        for sg in range(out_cols // (seg_mt * 128)):
            m0 = sg * seg_mt * 128
            slot, w = load_seg(wsrc[:, m0:m0 + seg_mt * 128].rearrange(
                "(k p) m -> p k m", p=128))
            sv = seg_view(slot, K, seg_mt * 128)
            lastmm = None
            for mt in range(seg_mt):
                j, ps, g = rot.next()
                for k in range(K):
                    waits = ([(DW, w), (V_, rhs_ready)] + g +
                             list(extra_mm_waits)) if k == 0 else ()
                    lastmm = sch.emit("tensor", lambda e, ps=ps, sv=sv, k=k,
                                      mt=mt, K=K: e.matmul(
                        ps[:], sv[:, k, mt * 128:(mt + 1) * 128],
                        rhs_fm[:, k, :], start=(k == 0), stop=(k == K - 1)),
                        waits=waits, inc=(P_ if k == K - 1 else None))
                sem, val = evac(sg * seg_mt + mt, ps, [(P_, lastmm)])
                rot.set_guard(j, sem, val)
            seg_done(slot, P_, lastmm)

    def transpose_tile(src_ap_fn, dst_fn, ready, n_jg=2, extra_first=(),
                       ev_sem="V"):
        for jg in range(n_jg):
            tp = None
            for jj in range(4):
                j = jg * 4 + jj
                w = ([ready, trg["tr"]] + list(extra_first)) if jj == 0 \
                    else [(P_, tp)]
                tp = sch.emit("tensor", lambda e, j=j, jj=jj, jg=jg: e.transpose(
                    psTR[:, jj * 128 + jg * 512:(jj + 1) * 128 + jg * 512],
                    src_ap_fn(j), idbf[:]), waits=w, inc=P_)
            ev = dst_fn(jg, [(P_, tp)])
            trg["tr"] = (ev_sem, ev)

    out_dx = 0

    st = [dict() for _ in range(NCH)]
    gd = {
        "xpar": [[("V", 0)], [("V", 0)]],
        "x0par": [[("V", 0)], [("V", 0)]],
        "xnfm": [("V", 0)],
        "zdt": [("V", 0)],
        "fmS": [("V", 0)],
        "xstore": [0, 0],
    }
    rngT = {"i": 0, "g": [("V", 0), ("V", 0)]}
    rngB = {"i": 0, "g": [("V", 0), ("V", 0)]}

    def phase_A(ch):
        t0 = ch * C
        par = ch % 2
        x_tm = x_tmd[:, par]
        xn0_tm = xn0_tmd[:, par]
        fw = list(gd["xpar"][par]) + [(DX, gd["xstore"][par])]
        if ch == 0:
            fw += [(V_, prep_v), (S_, prep_s), (P_, prep_p), (DW, prep_dw)]
        xw = dma("scalar", x_tm[:],
                 dx[t0:t0 + C, :].rearrange("(i p) m -> p i m", p=128), DX,
                 waits=fw)
        for i in range(NTOK):
            s1 = ln_norm(x_tm[:, i, :], xn0_tm[:, i, :],
                         extra_waits=[(DX, xw)] + gd["x0par"][par])
            sl = rngT["i"] % 2
            rngT["i"] += 1
            xs = tmA[:, sl, :]
            s2 = ln_norm(xn0_tm[:, i, :], xs, extra_waits=[(S_, s1),
                                                           rngT["g"][sl]])
            sch.emit("vector", lambda e, xs=xs: e.tensor_mul(
                out=xs, in0=xs, in1=nw_rep[:]), waits=[(S_, s2)], inc=None)
            xnv = sch.emit("vector", lambda e, xs=xs: e.tensor_add(
                out=xs, in0=xs, in1=nb_rep[:]), inc=V_)

            def dst_fn(jg, waits, i=i):
                return sch.emit("scalar", lambda e, i=i, jg=jg: e.activation(
                    out=xn_fm[:, jg * 4:(jg + 1) * 4, i * 128:(i + 1) * 128],
                    in_=psTR[:, jg * 512:(jg + 1) * 512].rearrange(
                        "p (a b) -> p a b", a=4), func=AF.Identity),
                    waits=waits, inc=S_)
            tp0 = sch.val(P_)
            transpose_tile(lambda j, xs=xs: xs[:, j * 128:(j + 1) * 128],
                           dst_fn, (V_, xnv), extra_first=gd["xnfm"],
                           ev_sem="S")
            rngT["g"][sl] = (P_, sch.val(P_))
        xnr = sch.emit("vector", lambda e: e.nop(),
                       waits=[(S_, sch.val(S_))], inc=V_)
        st[ch]["xn_ready"] = xnr
        st[ch]["xw"] = xw

    def phase_B(ch):
        xn_ready = st[ch]["xn_ready"]
        zdt_guard = gd["zdt"]

        def z_evac(mt, ps, waits):
            v = sch.emit("vector", lambda e, mt=mt, ps=ps: e.tensor_copy(
                out=z_fm[:, mt, :], in_=ps[:]),
                waits=waits + zdt_guard, inc=V_)
            return V_, v
        fm_mm(s_win, D, DI, xn_fm, z_evac, xn_ready, seg_mt=4)
        z_ready = sch.val(V_)

        j, ps, g = rot.next()
        lastmm = None
        for k in range(16):
            waits = ([(V_, z_ready), (V_, wbc_ready)] + g) if k == 0 else ()
            lastmm = sch.emit("tensor", lambda e, ps=ps, k=k: e.matmul(
                ps[0:32, :], wbc[:, k, :], z_fm[:, k, :],
                start=(k == 0), stop=(k == 15)),
                waits=waits, inc=(P_ if k == 15 else None))
        bc_ready = sch.emit("vector", lambda e, ps=ps: e.tensor_copy(
            out=bc_sb[:], in_=ps[0:32, :]), waits=[(P_, lastmm)], inc=V_)
        rot.set_guard(j, V_, bc_ready)

        sp_guard = {0: ("V", 0)}
        dt_rdy = [0] * 16

        def dt_evac(mt, ps, waits):
            s1 = sch.emit("scalar", lambda e, ps=ps, mt=mt: e.activation(
                out=sp_r[:, 0, :], in_=ps[:], func=AF.Exp,
                bias=bdt_sb[:, mt:mt + 1]), waits=waits + [sp_guard[0]], inc=S_)
            s2 = sch.emit("scalar", lambda e: e.activation(
                out=sp_r[:, 0, :], in_=sp_r[:, 0, :], func=AF.Ln,
                bias=one_col[:, 0:1]), inc=S_)
            v = sch.emit("vector", lambda e, mt=mt: e.tensor_scalar(
                out=dt_fm[:, mt, :], in0=sp_r[:, 0, :], scalar1=0.001,
                scalar2=1.0, op0=ALU.max, op1=ALU.min),
                waits=[(S_, s2)] + zdt_guard, inc=V_)
            sp_guard[0] = (V_, v)
            dt_rdy[mt] = v
            return S_, s1
        fm_mm(s_wdt, DI, DI, z_fm, dt_evac, z_ready, seg_mt=2)
        dt_ready = sch.val(V_)
        st[ch]["dt_rdy"] = dt_rdy

        def gate_evac(mt, ps, waits):
            s = sch.emit("scalar", lambda e, ps=ps, mt=mt: e.activation(
                out=gate_fm[:, mt, :], in_=ps[:], func=AF.Silu),
                waits=waits, inc=S_)
            return S_, s
        fm_mm(s_wgate, D, DI, xn_fm, gate_evac, xn_ready, seg_mt=4)
        gate_ready_s = sch.val(S_)
        gd["xnfm"] = [(P_, sch.val(P_))]

        st[ch].update(z_ready=z_ready, dt_ready=dt_ready, bc_ready=bc_ready,
                      gate_ready_s=gate_ready_s)
        if ch == 0 and "z_fm" in dbg_outs:
            dma("sync", dbg_outs["z_fm"][:], z_fm, DX, waits=[(V_, z_ready)])
        if ch == 0 and "dt_fm" in dbg_outs:
            dma("sync", dbg_outs["dt_fm"][:], dt_fm, DX, waits=[(V_, dt_ready)])
        if ch == 0 and "gate_fm" in dbg_outs:
            dma("sync", dbg_outs["gate_fm"][:], gate_fm[:], DX,
                waits=[(S_, gate_ready_s)])
        if ch == 0 and "bc_sb" in dbg_outs:
            dma("sync", dbg_outs["bc_sb"][:], bc_sb[:], DX,
                waits=[(V_, bc_ready)])
        if ch == 0 and "xn_fm" in dbg_outs:
            dma("sync", dbg_outs["xn_fm"][:], xn_fm[:], DX,
                waits=[(V_, xn_ready)])

    def phase_scan(ch):
        dt_rdy = st[ch]["dt_rdy"]
        bc_ready = st[ch]["bc_ready"]
        u_rdy = [0] * 16
        for dc in range(16):
            u_rdy[dc] = sch.emit("vector", lambda e, dc=dc: e.tensor_mul(
                out=z_fm[:, dc, :], in0=z_fm[:, dc, :], in1=dt_fm[:, dc, :]),
                waits=[(V_, dt_rdy[dc])], inc=V_)

        NGP = 0
        pend_carry = []
        cm_guard = [("V", 0), ("V", 0)]
        ring_g = [[("V", 0)], [("V", 0)]]
        for s in range(16):
            prev_sweep = [(V_, sch.val(V_)), (G_, sch.val(G_))]
            j, ps, g = rot.next()
            pm = sch.emit("tensor", lambda e, ps=ps, s=s: e.matmul(
                ps[:], sel[:, s * 128:(s + 1) * 128], bc_sb[:],
                start=True, stop=True), waits=[(V_, bc_ready)] + g, inc=P_)
            bv = sch.emit("scalar", lambda e, ps=ps: e.activation(
                out=b_bc[:], in_=ps[:], func=AF.Identity),
                waits=[(P_, pm)] + prev_sweep, inc=S_)
            rot.set_guard(j, S_, bv)
            j, ps, g = rot.next()
            pm = sch.emit("tensor", lambda e, ps=ps, s=s: e.matmul(
                ps[:], sel[:, (16 + s) * 128:(17 + s) * 128], bc_sb[:],
                start=True, stop=True), waits=g, inc=P_)
            cv = sch.emit("scalar", lambda e, ps=ps: e.activation(
                out=c_bc[:], in_=ps[:], func=AF.Identity),
                waits=[(P_, pm)] + prev_sweep, inc=S_)
            rot.set_guard(j, S_, cv)

            for dc in range(16):
                it = s * 16 + dc
                sl = it % 2
                col = s * 16 + dc
                sa = sch.emit("scalar", lambda e, sl=sl, dc=dc, s=s: e.activation(
                    out=dAr[:, sl, :], in_=dt_fm[:, dc, :], func=AF.Exp,
                    scale=a_sb[:, dc, s:s + 1]),
                    waits=ring_g[sl] + [(V_, dt_rdy[dc]), (V_, a_ready)], inc=S_)
                if len(pend_carry) >= 2:
                    psl, pcol = pend_carry.pop(0)
                    sch.emit("vector", lambda e, psl=psl, pcol=pcol:
                             e.tensor_copy(out=H[:, pcol:pcol + 1],
                                           in_=hr[:, psl, C - 1:C]), inc=None)
                sch.emit("vector", lambda e, sl=sl, dc=dc: e.tensor_mul(
                    out=br_s[:], in0=z_fm[:, dc, :], in1=b_bc[:]),
                    waits=[(V_, u_rdy[dc]), (S_, bv)], inc=None)
                hv = sch.emit("vector", lambda e, sl=sl, col=col:
                              e.tensor_tensor_scan(
                    out=hr[:, sl, :], data0=dAr[:, sl, :], data1=br_s[:],
                    initial=H[:, col:col + 1], op0=ALU.mult, op1=ALU.add),
                    waits=[(S_, sa)], inc=V_)
                pend_carry.append((sl, col))
                if dc < NGP:
                    cm = sch.emit("vector", lambda e, sl=sl: e.tensor_mul(
                        out=cmr2[:, sl, :], in0=hr[:, sl, :], in1=c_bc[:]),
                        waits=[(S_, cv), cm_guard[sl]], inc=V_)
                    if s == 0:
                        gy = sch.emit("gpsimd", lambda e, sl=sl, dc=dc:
                                      e.tensor_copy(
                            out=y_fm[:, dc, :], in_=cmr2[:, sl, :]),
                            waits=[(V_, cm)], inc=G_)
                    else:
                        gy = sch.emit("gpsimd", lambda e, sl=sl, dc=dc:
                                      e.tensor_add(
                            out=y_fm[:, dc, :], in0=y_fm[:, dc, :],
                            in1=cmr2[:, sl, :]), waits=[(V_, cm)], inc=G_)
                    cm_guard[sl] = (G_, gy)
                else:
                    if s == 0:
                        sch.emit("vector", lambda e, sl=sl, dc=dc: e.tensor_mul(
                            out=y_fm[:, dc, :], in0=hr[:, sl, :], in1=c_bc[:]),
                            waits=[(S_, cv)], inc=None)
                    else:
                        sch.emit("vector", lambda e, sl=sl: e.tensor_mul(
                            out=br_s[:], in0=hr[:, sl, :], in1=c_bc[:]),
                            waits=[(S_, cv)], inc=None)
                        sch.emit("vector", lambda e, dc=dc: e.tensor_add(
                            out=y_fm[:, dc, :], in0=y_fm[:, dc, :],
                            in1=br_s[:]), inc=None)
                ring_g[sl] = [(V_, hv)]
        while pend_carry:
            psl, pcol = pend_carry.pop()
            vv_ = sch.val(V_)
            sch.emit("vector", lambda e, v=vv_: e.wait_ge(sch.sems["V"], v))
            sch.emit("vector", lambda e, psl=psl, pcol=pcol: e.tensor_copy(
                out=H[:, pcol:pcol + 1], in_=hr[:, psl, C - 1:C]), inc=None)
        y_ready = sch.emit("vector", lambda e: e.nop(),
                           waits=[(G_, sch.val(G_))], inc=V_)
        st[ch]["y_ready"] = y_ready
        if ch == 0 and "y_fm" in dbg_outs:
            dma("sync", dbg_outs["y_fm"][:], y_fm[:], DX, waits=[(V_, y_ready)])

    def phase_D(ch):
        par = ch % 2
        x_tm = x_tmd[:, par]
        xn0_tm = xn0_tmd[:, par]
        y_ready = st[ch]["y_ready"]
        gate_ready_s = st[ch]["gate_ready_s"]
        for k in range(16):
            sch.emit("vector", lambda e, k=k: e.tensor_mul(
                out=gate_fm[:, k, :], in0=y_fm[:, k, :], in1=gate_fm[:, k, :]),
                waits=([(S_, gate_ready_s), (V_, y_ready)] if k == 0 else ()),
                inc=(V_ if k == 15 else None))
        yg_ready = sch.val(V_)

        def attn_evac(mt, ps, waits):
            v = sch.emit("scalar", lambda e, mt=mt, ps=ps: e.activation(
                out=fmS[:, mt, :], in_=ps[:], func=AF.Identity),
                waits=waits, inc=S_)
            return S_, v
        fm_mm(s_wout, DI, D, gate_fm, attn_evac, yg_ready, seg_mt=2)
        attn_ready_s = sch.val(S_)
        attn_ready = sch.emit("vector", lambda e: e.nop(),
                              waits=[(S_, attn_ready_s)], inc=V_)
        if ch == 0 and "attn_fm" in dbg_outs:
            dma("sync", dbg_outs["attn_fm"][:], fmS[:], DX,
                waits=[(V_, attn_ready)])

        x1v = 0
        for i in range(NTOK):
            sl = rngB["i"] % 2
            rngB["i"] += 1
            rb = tmB[:, sl, :]

            def dst_fn(jg, waits, i=i, rb=rb):
                return sch.emit("vector", lambda e, i=i, jg=jg, rb=rb:
                                e.tensor_add(
                    out=rb[:, jg * 512:(jg + 1) * 512].rearrange(
                        "p (a b) -> p a b", a=4),
                    in0=psTR[:, jg * 512:(jg + 1) * 512].rearrange(
                        "p (a b) -> p a b", a=4),
                    in1=xn0_tm[:, i, jg * 512:(jg + 1) * 512].rearrange(
                        "p (a b) -> p a b", a=4)),
                    waits=waits + ([rngB["g"][sl]] if jg == 0 else []), inc=V_)
            transpose_tile(lambda j, i=i: fmS[:, j, i * 128:(i + 1) * 128],
                           dst_fn, (V_, attn_ready))
            sl2 = rngT["i"] % 2
            rngT["i"] += 1
            ra = tmA[:, sl2, :]
            s3 = ln_norm(rb, ra, extra_waits=[(V_, sch.val(V_)),
                                              rngT["g"][sl2]])
            rngB["g"][sl] = (S_, s3)
            x1v = sch.emit("vector", lambda e, i=i, ra=ra: e.tensor_add(
                out=x_tm[:, i, :], in0=x_tm[:, i, :], in1=ra),
                waits=[(S_, s3)], inc=V_)
            rngT["g"][sl2] = (V_, x1v)
        st[ch]["x1_ready"] = x1v
        gd["x0par"][par] = [(V_, x1v)]
        if ch == 0 and "x1_tm" in dbg_outs:
            dma("sync", dbg_outs["x1_tm"][:], x_tm[:], DX, waits=[(V_, x1v)])

    def phase_E(ch):
        par = ch % 2
        x_tm = x_tmd[:, par]
        x1_ready = st[ch]["x1_ready"]
        xqv = 0
        for i in range(NTOK):
            sl = rngB["i"] % 2
            rngB["i"] += 1
            rb = tmB[:, sl, :]
            s4 = ln_norm(x_tm[:, i, :], rb,
                         extra_waits=[(V_, x1_ready), rngB["g"][sl]])
            mx = stat[:, 15:16]
            sc4 = stat[:, 16:17]
            vr = sch.emit("vector", lambda e, rb=rb, mx=mx: e.reduce_max(
                out=mx, in_=rb, axis=AX.X, apply_absolute_value=True),
                waits=[(S_, s4)], inc=V_)
            sch.emit("vector", lambda e, v=vr: e.wait_ge(sch.sems["V"], v))
            vr = sch.emit("vector", lambda e, mx=mx, sc4=sc4: e.reciprocal(
                out=sc4, in_=mx), inc=V_)
            sch.emit("vector", lambda e, v=vr: e.wait_ge(sch.sems["V"], v))
            vv = sch.emit("vector", lambda e, sc4=sc4: e.tensor_scalar(
                out=sc4, in0=sc4, scalar1=127.0, scalar2=None, op0=ALU.mult),
                inc=V_)
            sch.emit("vector", lambda e, mx=mx, i=i: e.tensor_scalar(
                out=stat[:, 17 + i:18 + i], in0=mx, scalar1=1.0 / 127.0,
                scalar2=None, op0=ALU.mult), inc=None)
            s5 = sch.emit("scalar", lambda e, rb=rb, sc4=sc4: e.activation(
                out=f32r[0][:], in_=rb[:, 0:512], func=AF.Identity,
                scale=sc4, bias=magic_col[:, 0:1]), waits=[(V_, vv)], inc=S_)
            s6 = sch.emit("scalar", lambda e, rb=rb, sc4=sc4: e.activation(
                out=f32r[1][:], in_=rb[:, 512:1024], func=AF.Identity,
                scale=sc4, bias=magic_col[:, 0:1]), inc=S_)
            sch.emit("vector", lambda e, rb=rb: e.tensor_scalar(
                out=rb[:, 0:512], in0=f32r[0][:], scalar1=-MAGIC,
                scalar2=None, op0=ALU.add), waits=[(S_, s5)], inc=None)
            xqv = sch.emit("vector", lambda e, rb=rb: e.tensor_scalar(
                out=rb[:, 512:1024], in0=f32r[1][:], scalar1=-MAGIC,
                scalar2=None, op0=ALU.add), waits=[(S_, s6)], inc=V_)
            tp = sch.emit("tensor", lambda e, i=i: e.transpose(
                psTRf[0:1, 0:128], stat[:, 17 + i:18 + i], idf[:]),
                waits=[(V_, xqv), trg["trf"]], inc=P_)
            rv = sch.emit("vector", lambda e, i=i: e.tensor_copy(
                out=row_f[0:1, i * 128:(i + 1) * 128], in_=psTRf[0:1, 0:128]),
                waits=[(P_, tp)], inc=V_)
            trg["trf"] = (V_, rv)

            def dst_fn(jg, waits, i=i):
                return sch.emit("scalar", lambda e, i=i, jg=jg: e.activation(
                    out=fmS[:, jg * 4:(jg + 1) * 4, i * 128:(i + 1) * 128],
                    in_=psTR[:, jg * 512:(jg + 1) * 512].rearrange(
                        "p (a b) -> p a b", a=4), func=AF.Identity),
                    waits=waits, inc=S_)
            transpose_tile(lambda j, rb=rb: rb[:, j * 128:(j + 1) * 128],
                           dst_fn, (V_, xqv), ev_sem="S")
            rngB["g"][sl] = (P_, sch.val(P_))
        xhat_ready = sch.emit("vector", lambda e: e.nop(),
                              waits=[(S_, sch.val(S_))], inc=V_)
        invsx_row_ready = sch.val(V_)
        if ch == 0 and "xhat_fm" in dbg_outs:
            dma("sync", dbg_outs["xhat_fm"][:], fmS[:], DX,
                waits=[(V_, xhat_ready)])

        j, ps, g = rot.next()
        pm = sch.emit("tensor", lambda e, ps=ps: e.matmul(
            ps[:], ones_r_bf[:], row_f[0:1, 0:C], start=True, stop=True),
            waits=[(V_, invsx_row_ready)] + g, inc=P_)
        invsx_ready = sch.emit("vector", lambda e, ps=ps: e.tensor_copy(
            out=invsx_bc[:], in_=ps[:]), waits=[(P_, pm)], inc=V_)
        rot.set_guard(j, V_, invsx_ready)

        sch.emit("vector", lambda e: e.memset(mhi[:], -1e30), inc=None)
        mm0 = sch.emit("vector", lambda e: e.memset(mlo[:], 1e30), inc=V_)

        gu_state = {"sgs": ("S", 0)}
        for q in range(8):
            m0 = q * 512
            slot_g, wg = load_seg(s_fgT[:, m0:m0 + 512].rearrange(
                "(k p) m -> p k m", p=128))
            slot_u, wu = load_seg(s_fuT[:, m0:m0 + 512].rearrange(
                "(k p) m -> p k m", p=128))
            svg = seg_view(slot_g, 8, 512)
            svu = seg_view(slot_u, 8, 512)
            glast = ulast = None
            for mt in range(4):
                gm = q * 4 + mt
                jg_, psg, gg = rot.next()
                for k in range(8):
                    waits = ([(DW, wg), (V_, xhat_ready)] + gg) if k == 0 else ()
                    glast = sch.emit("tensor", lambda e, ps=psg, sv=svg, k=k,
                                     mt=mt: e.matmul(
                        ps[:], sv[:, k, mt * 128:(mt + 1) * 128], fmS[:, k, :],
                        start=(k == 0), stop=(k == 7)),
                        waits=waits, inc=(P_ if k == 7 else None))
                ju_, psu, gu_ = rot.next()
                for k in range(8):
                    waits = ([(DW, wu)] + gu_) if k == 0 else ()
                    ulast = sch.emit("tensor", lambda e, ps=psu, sv=svu, k=k,
                                     mt=mt: e.matmul(
                        ps[:], sv[:, k, mt * 128:(mt + 1) * 128], fmS[:, k, :],
                        start=(k == 0), stop=(k == 7)),
                        waits=waits, inc=(P_ if k == 7 else None))
                v1 = sch.emit("vector", lambda e, ps=psg: e.tensor_mul(
                    out=f32r[0][:], in0=ps[:], in1=invsx_bc[:]),
                    waits=[(P_, glast), (V_, invsx_ready), (V_, mm0),
                           gu_state["sgs"]], inc=V_)
                rot.set_guard(jg_, V_, v1)
                sgs = sch.emit("scalar", lambda e: e.activation(
                    out=sg_b[:], in_=f32r[0][:], func=AF.Silu,
                    scale=wgs_bc[:, 0:1]), waits=[(V_, v1)], inc=S_)
                gu_state["sgs"] = (S_, sgs)
                v2 = sch.emit("vector", lambda e, ps=psu: e.tensor_mul(
                    out=f32r[1][:], in0=ps[:], in1=invsx_bc[:]),
                    waits=[(P_, ulast)], inc=V_)
                rot.set_guard(ju_, V_, v2)
                sch.emit("vector", lambda e, gm=gm: e.tensor_mul(
                    out=gu_fm[:, gm, :], in0=f32r[1][:], in1=sg_b[:]),
                    waits=[(S_, sgs)], inc=None)
                sch.emit("vector", lambda e, gm=gm: e.tensor_max(
                    out=mhi[:], in0=mhi[:], in1=gu_fm[:, gm, :]), inc=None)
                sch.emit("vector", lambda e, gm=gm: e.tensor_tensor(
                    out=mlo[:], in0=mlo[:], in1=gu_fm[:, gm, :], op=ALU.min),
                    inc=(V_ if mt == 3 else None))
            seg_done(slot_g, P_, glast)
            seg_done(slot_u, P_, ulast)
        gu_ready = sch.val(V_)
        if ch == 0 and "gu_fm" in dbg_outs:
            dma("sync", dbg_outs["gu_fm"][:], gu_fm[:], DX,
                waits=[(V_, gu_ready)])

        sch.emit("vector", lambda e: e.tensor_scalar(
            out=mlo[:], in0=mlo[:], scalar1=-1.0, scalar2=None, op0=ALU.mult),
            inc=None)
        mab = sch.emit("vector", lambda e: e.tensor_max(
            out=mhi[:], in0=mhi[:], in1=mlo[:]), inc=V_)

        mtp = 0
        for i in range(NTOK):
            tp = sch.emit("tensor", lambda e, i=i: e.transpose(
                psTR[:, 0:128], mhi[:, i * 128:(i + 1) * 128], idbf[:]),
                waits=[(V_, mab), trg["tr"]], inc=P_)
            ev = sch.emit("vector", lambda e: e.tensor_copy(
                out=sg_b[:, 0:128], in_=psTR[:, 0:128]),
                waits=[(P_, tp), (P_, mtp)], inc=V_)
            trg["tr"] = (V_, ev)

            def vstep(fn, waits=()):
                v = sch.emit("vector", fn, waits=waits, inc=V_)
                sch.emit("vector", lambda e, v=v: e.wait_ge(sch.sems["V"], v))
                return v
            vstep(lambda e: e.reduce_max(
                out=stat[:, 21:22], in_=sg_b[:, 0:128], axis=AX.X))
            vstep(lambda e: e.reciprocal(
                out=stat[:, 22:23], in_=stat[:, 21:22]))
            vstep(lambda e: e.tensor_scalar(
                out=stat[:, 22:23], in0=stat[:, 22:23], scalar1=127.0,
                scalar2=None, op0=ALU.mult))
            vstep(lambda e: e.tensor_mul(
                out=stat[:, 21:22], in0=stat[:, 21:22], in1=wus_bc[:]))
            vstep(lambda e: e.tensor_mul(
                out=stat[:, 21:22], in0=stat[:, 21:22], in1=wds_bc[:]))
            cu = sch.emit("vector", lambda e: e.tensor_scalar(
                out=stat[:, 21:22], in0=stat[:, 21:22], scalar1=1.0 / 127.0,
                scalar2=None, op0=ALU.mult), inc=V_)
            for cidx, off in [(22, C), (21, 2 * C)]:
                tp = sch.emit("tensor", lambda e, cidx=cidx: e.transpose(
                    psTRf[0:1, 0:128], stat[:, cidx:cidx + 1], idf[:]),
                    waits=[(V_, cu), trg["trf"]], inc=P_)
                rv = sch.emit("vector", lambda e, i=i, off=off: e.tensor_copy(
                    out=row_f[0:1, off + i * 128:off + (i + 1) * 128],
                    in_=psTRf[0:1, 0:128]), waits=[(P_, tp)], inc=V_)
                trg["trf"] = (V_, rv)
            mtp = sch.val(P_)
        rows_ready = sch.val(V_)

        for dst, off in [(comb_bc, C), (unsc_bc, 2 * C)]:
            j, ps, g = rot.next()
            pm = sch.emit("tensor", lambda e, ps=ps, off=off: e.matmul(
                ps[:], ones_r_bf[:], row_f[0:1, off:off + C], start=True,
                stop=True), waits=[(V_, rows_ready)] + g, inc=P_)
            ev = sch.emit("vector", lambda e, ps=ps, dst=dst: e.tensor_copy(
                out=dst[:], in_=ps[:]), waits=[(P_, pm)], inc=V_)
            rot.set_guard(j, V_, ev)
        comb_ready = sch.val(V_)

        f32g = [("S", 0), ("S", 0)]
        for gm in range(32):
            v1 = sch.emit("vector", lambda e, gm=gm: e.tensor_mul(
                out=f32r[gm % 2][:], in0=gu_fm[:, gm, :], in1=comb_bc[:]),
                waits=([(V_, comb_ready)] if gm < 2 else []) + [f32g[gm % 2]],
                inc=V_)
            sq = sch.emit("scalar", lambda e, gm=gm: e.activation(
                out=f32r[gm % 2][:], in_=f32r[gm % 2][:], func=AF.Identity,
                bias=magic_col[:, 0:1]), waits=[(V_, v1)], inc=S_)
            sch.emit("scalar", lambda e, v=sq: e.wait_ge(sch.sems["S"], v))
            sb2 = sch.emit("scalar", lambda e, gm=gm: e.activation(
                out=gu_fm[:, gm, :], in_=f32r[gm % 2][:], func=AF.Identity,
                bias=negmagic_col[:, 0:1]), inc=S_)
            f32g[gm % 2] = (S_, sb2)
        x2h_s = sch.val(S_)
        x2h_ready = sch.emit("vector", lambda e: e.nop(),
                             waits=[(S_, x2h_s)], inc=V_)
        if ch == 0 and "x2hat_fm" in dbg_outs:
            dma("sync", dbg_outs["x2hat_fm"][:], gu_fm[:], DX,
                waits=[(V_, x2h_ready)])

        def down_evac(mt, ps, waits):
            v = sch.emit("vector", lambda e, mt=mt, ps=ps: e.tensor_mul(
                out=f_fm[:, mt, :], in0=ps[:], in1=unsc_bc[:]),
                waits=waits, inc=V_)
            return V_, v
        fm_mm(s_fdT, INNER, D, gu_fm, down_evac, x2h_ready, seg_mt=1,
              extra_mm_waits=[(V_, comb_ready)])
        f_ready = sch.val(V_)
        gd["zdt"] = [(P_, sch.val(P_))]
        if ch == 0 and "f_fm" in dbg_outs:
            dma("sync", dbg_outs["f_fm"][:], f_fm[:], DX, waits=[(V_, f_ready)])

        t0 = ch * C
        xv = 0
        for i in range(NTOK):
            sl = rngB["i"] % 2
            rngB["i"] += 1
            rb = tmB[:, sl, :]

            def dst_fn(jg, waits, i=i, rb=rb):
                return sch.emit("scalar", lambda e, jg=jg, rb=rb: e.activation(
                    out=rb[:, jg * 512:(jg + 1) * 512],
                    in_=psTR[:, jg * 512:(jg + 1) * 512], func=AF.Identity),
                    waits=waits + ([rngB["g"][sl]] if jg == 0 else []), inc=S_)
            transpose_tile(lambda j, i=i: f_fm[:, j, i * 128:(i + 1) * 128],
                           dst_fn, (V_, f_ready), ev_sem="S")
            sl2 = rngT["i"] % 2
            rngT["i"] += 1
            ra = tmA[:, sl2, :]
            s7 = ln_norm(rb, ra, extra_waits=[(S_, sch.val(S_)),
                                              rngT["g"][sl2]])
            rngB["g"][sl] = (S_, s7)
            xv = sch.emit("vector", lambda e, i=i, ra=ra: e.tensor_add(
                out=x_tm[:, i, :], in0=x_tm[:, i, :], in1=ra),
                waits=[(S_, s7)], inc=V_)
            rngT["g"][sl2] = (V_, xv)
            gd["xstore"][par] = dma(
                "scalar", d_out[t0 + i * 128:t0 + (i + 1) * 128, :],
                x_tm[:, i, :], DX, waits=[(V_, xv)])
        gd["xpar"][par] = [(V_, xv)]

    nch_run = NCH if _STAGE >= 7 else (0 if _STAGE < 2 else 1)
    for ch in range(nch_run):
        phase_A(ch)
        if ch > 0 and _STAGE >= 5:
            phase_D(ch - 1)
            if _STAGE >= 6:
                phase_E(ch - 1)
        if _STAGE >= 3:
            phase_B(ch)
        if _STAGE >= 4:
            phase_scan(ch)
        if ch == 0 and "xn0_tm" in dbg_outs:
            dma("sync", dbg_outs["xn0_tm"][:], xn0_tmd[:, 0], DX,
                waits=[(V_, st[0]["xn_ready"]), (DW, sch.val(DW))])
    if nch_run and _STAGE >= 5:
        phase_D(nch_run - 1)
        if _STAGE >= 6:
            phase_E(nch_run - 1)

    sch.emit("sync", lambda e: None, waits=[(DX, sch.val(DX)),
                                            (DW, sch.val(DW))])
    sch.realize(block)
    es.close()
    return nc, sorted(dbg_outs.keys())


def kernel(L0_attn_x, L0_attn_norm_w, L0_attn_norm_b, L0_attn_W_in,
           L0_attn_W_gate, L0_attn_W_dt, L0_attn_b_dt, L0_attn_W_B,
           L0_attn_W_C, L0_attn_A_log, L0_attn_W_out, L0_ffn_w_gate,
           L0_ffn_w_up, L0_ffn_w_down):
    nc, dbg_names = build_program()

    sel = np.zeros((32, 32 * 128), dtype=BFNP)
    for s in range(32):
        sel[s, s * 128:(s + 1) * 128] = 1

    common = {
        "norm_w": np.ascontiguousarray(L0_attn_norm_w),
        "norm_b": np.ascontiguousarray(L0_attn_norm_b),
        "W_in": np.ascontiguousarray(L0_attn_W_in),
        "W_gate": np.ascontiguousarray(L0_attn_W_gate),
        "W_dt": np.ascontiguousarray(L0_attn_W_dt),
        "b_dt": np.ascontiguousarray(L0_attn_b_dt),
        "W_B": np.ascontiguousarray(L0_attn_W_B),
        "W_C": np.ascontiguousarray(L0_attn_W_C),
        "A_log": np.ascontiguousarray(L0_attn_A_log),
        "W_out": np.ascontiguousarray(L0_attn_W_out),
        "ffn_w_gate": np.ascontiguousarray(L0_ffn_w_gate),
        "ffn_w_up": np.ascontiguousarray(L0_ffn_w_up),
        "ffn_w_down": np.ascontiguousarray(L0_ffn_w_down),
        "ident_bf": np.eye(128, dtype=BFNP),
        "ident_f32": np.eye(128, dtype=np.float32),
        "ones_r_bf": np.ones((1, 128), dtype=BFNP),
        "ones_r_f": np.ones((1, 128), dtype=np.float32),
        "ones_c_f": np.ones((128, 1), dtype=np.float32),
        "sel32": sel,
    }
    x = np.ascontiguousarray(L0_attn_x)
    in_maps = [dict(common, x=np.ascontiguousarray(x[b])) for b in range(8)]
    res = run_bass_kernel_spmd(nc, in_maps, core_ids=list(range(8)))
    out = np.stack([res.results[b]["out"] for b in range(8)], axis=0)
    kernel._results = res.results
    return out


# version 14 (final)
# speedup vs baseline: 1.0212x; 1.0180x over previous
import numpy as np
import ml_dtypes
from contextlib import ExitStack

import concourse.bass as bass
import concourse.mybir as mybir
from concourse.bass_utils import run_bass_kernel_spmd

F32 = mybir.dt.float32
BF16 = mybir.dt.bfloat16
AF = mybir.ActivationFunctionType
ALU = mybir.AluOpType
AX = mybir.AxisListType
BFNP = ml_dtypes.bfloat16

T = 4096
D = 1024
DI = 2048
DS = 16
INNER = 4096
C = 512
NCH = T // C
NTOK = C // 128
MAGIC = 12582912.0
LN_EPS = 1e-5

ENGINES = ("tensor", "vector", "scalar", "gpsimd", "sync")

_DBG = set()
_STAGE = 99
_PREP = 9


class Sched:
    def __init__(self, nc, es):
        self.nc = nc
        self.q = {e: [] for e in ENGINES}
        self.cnt = {}
        self.sems = {}
        for name in ("P", "V", "S", "G", "DW", "DX"):
            self.sems[name] = es.enter_context(nc.semaphore(name))
            self.cnt[name] = 0

    def emit(self, eng, fn, waits=(), inc=None, inc_n=1):
        w = tuple((s, v) for (s, v) in waits if v > 0)
        if inc is not None:
            self.cnt[inc] += inc_n
        self.q[eng].append((fn, w, inc, inc_n))
        return self.cnt[inc] if inc is not None else None

    def val(self, s):
        return self.cnt[s]

    def realize(self, block):
        sems = self.sems
        for eng in ENGINES:
            ops = self.q[eng]

            def make(ops):
                def body(e):
                    last = {}
                    for fn, waits, inc, inc_n in ops:
                        for s, v in waits:
                            if v > last.get(s, 0):
                                e.wait_ge(sems[s], v)
                                last[s] = v
                        inst = fn(e)
                        if inc is not None:
                            if inst is None:
                                inst = e.nop()
                            inst.then_inc(sems[inc], inc_n)
                return body

            getattr(block, eng)(make(ops))


def build_program():
    nc = bass.Bass("TRN2", target_bir_lowering=False)
    es = ExitStack()

    dx = nc.dram_tensor("x", [T, D], F32, kind="ExternalInput").ap()
    d_nw = nc.dram_tensor("norm_w", [D], F32, kind="ExternalInput").ap()
    d_nb = nc.dram_tensor("norm_b", [D], F32, kind="ExternalInput").ap()
    d_win = nc.dram_tensor("W_in", [D, DI], F32, kind="ExternalInput").ap()
    d_wgate = nc.dram_tensor("W_gate", [D, DI], F32, kind="ExternalInput").ap()
    d_wdt = nc.dram_tensor("W_dt", [DI, DI], F32, kind="ExternalInput").ap()
    d_bdt = nc.dram_tensor("b_dt", [DI], F32, kind="ExternalInput").ap()
    d_wB = nc.dram_tensor("W_B", [DI, DS], F32, kind="ExternalInput").ap()
    d_wC = nc.dram_tensor("W_C", [DI, DS], F32, kind="ExternalInput").ap()
    d_alog = nc.dram_tensor("A_log", [DI, DS], F32, kind="ExternalInput").ap()
    d_wout = nc.dram_tensor("W_out", [DI, D], F32, kind="ExternalInput").ap()
    d_fg = nc.dram_tensor("ffn_w_gate", [INNER, D], F32, kind="ExternalInput").ap()
    d_fu = nc.dram_tensor("ffn_w_up", [INNER, D], F32, kind="ExternalInput").ap()
    d_fd = nc.dram_tensor("ffn_w_down", [D, INNER], F32, kind="ExternalInput").ap()
    d_idbf = nc.dram_tensor("ident_bf", [128, 128], BF16, kind="ExternalInput").ap()
    d_idf = nc.dram_tensor("ident_f32", [128, 128], F32, kind="ExternalInput").ap()
    d_ones_r_bf = nc.dram_tensor("ones_r_bf", [1, 128], BF16, kind="ExternalInput").ap()
    d_ones_r_f = nc.dram_tensor("ones_r_f", [1, 128], F32, kind="ExternalInput").ap()
    d_ones_c_f = nc.dram_tensor("ones_c_f", [128, 1], F32, kind="ExternalInput").ap()
    d_sel = nc.dram_tensor("sel32", [32, 32 * 128], BF16, kind="ExternalInput").ap()

    d_out = nc.dram_tensor("out", [T, D], F32, kind="ExternalOutput").ap()

    s_win = nc.dram_tensor("s_win", [D, DI], BF16, kind="Internal").ap()
    s_wgate = nc.dram_tensor("s_wgate", [D, DI], BF16, kind="Internal").ap()
    s_wdt = nc.dram_tensor("s_wdt", [DI, DI], BF16, kind="Internal").ap()
    s_wout = nc.dram_tensor("s_wout", [DI, D], BF16, kind="Internal").ap()
    s_fgT = nc.dram_tensor("s_fgT", [D, INNER], BF16, kind="Internal").ap()
    s_fuT = nc.dram_tensor("s_fuT", [D, INNER], BF16, kind="Internal").ap()
    s_fdT = nc.dram_tensor("s_fdT", [INNER, D], BF16, kind="Internal").ap()

    dbg_outs = {}
    for name, shape, dt in [
        ("xn0_tm", [128, NTOK, D], BF16), ("xn_fm", [128, 8, C], BF16),
        ("z_fm", [128, 16, C], BF16), ("dt_fm", [128, 16, C], BF16),
        ("gate_fm", [128, 16, C], BF16), ("bc_sb", [32, C], BF16),
        ("y_fm", [128, 16, C], BF16), ("attn_fm", [128, 8, C], BF16),
        ("x1_tm", [128, NTOK, D], F32), ("xhat_fm", [128, 8, C], BF16),
        ("gu_fm", [128, 32, C], BF16), ("x2hat_fm", [128, 32, C], BF16),
        ("wq_probe", [128, 512], BF16), ("a_probe", [128, 16, 16], F32),
        ("tmA", [128, NTOK, D], BF16), ("wq2_probe", [128, 512], BF16),
        ("f_fm", [128, 8, C], BF16),
        ("scales_probe", [128, 8], F32),
    ]:
        if name in _DBG:
            dbg_outs[name] = nc.dram_tensor("dbg_" + name, shape, dt,
                                            kind="ExternalOutput").ap()

    sch = Sched(nc, es)
    P_, V_, S_, G_, DW, DX = "P", "V", "S", "G", "DW", "DX"

    import numpy as _np
    _alloc_log = []

    def SBe(e_, n, s, d):
        bpp = int(_np.prod(s[1:])) * mybir.dt.size(d)
        _alloc_log.append((n, bpp))
        return e_.enter_context(nc.sbuf_tensor(n, s, d))
    SB = lambda n, s, d: SBe(es, n, s, d)
    build_program._alloc_log = _alloc_log
    PSUM = lambda n, s, d: es.enter_context(nc.psum_tensor(n, s, d))

    idbf = SB("sb_idbf", [128, 128], BF16)
    idf = SB("sb_idf", [128, 128], F32)
    ones_r_bf = SB("sb_ones_r_bf", [1, 128], BF16)
    ones_r_f = SB("sb_ones_r_f", [1, 128], F32)
    ones_c_f = SB("sb_ones_c_f", [128, 1], F32)
    sel = SB("sb_sel", [32, 32 * 128], BF16)
    nw_rep = SB("nw_rep", [128, D], BF16)
    nb_rep = SB("nb_rep", [128, D], BF16)
    a_sb = SB("a_sb", [128, 16, 16], F32)
    bdt_sb = SB("bdt_sb", [128, 16], F32)
    wbc = SB("wbc", [128, 16, 32], BF16)
    wgs_bc = SB("wgs_bc", [128, 1], F32)
    wus_bc = SB("wus_bc", [128, 1], F32)
    wds_bc = SB("wds_bc", [128, 1], F32)
    wgr_bc = SB("wgr_bc", [128, 1], F32)
    wur_bc = SB("wur_bc", [128, 1], F32)
    wdr_bc = SB("wdr_bc", [128, 1], F32)
    H = SB("H", [128, 256], F32)
    magic_col = SB("magic_col", [128, 1], F32)
    eps_col = SB("eps_col", [128, 1], F32)
    one_col = SB("one_col", [128, 1], F32)
    neghalf_col = SB("neghalf_col", [128, 1], F32)
    negmagic_col = SB("negmagic_col", [128, 1], F32)

    NSLOT = 2
    SEGSZ = 4096
    wslot = [SB(f"wslot{j}", [128, SEGSZ], BF16) for j in range(NSLOT)]

    psROT = [PSUM(f"ps{j}", [128, 512], F32) for j in range(5)]
    psTR = PSUM("psTR", [128, 1024], BF16)
    psTRf = PSUM("psTRf", [128, 512], F32)

    block = es.enter_context(nc.Block())

    class Rot:
        def __init__(self, banks):
            self.banks = banks
            self.i = 0
            self.guard = [("V", 0)] * len(banks)

        def next(self):
            j = self.i % len(self.banks)
            self.i += 1
            return j, self.banks[j], [self.guard[j]]

        def set_guard(self, j, sem, val):
            self.guard[j] = (sem, val)

    rot = Rot(psROT)
    trg = {"tr": ("V", 0), "trf": ("V", 0)}

    def pace(eng="tensor"):
        sch.emit(eng, lambda e: e.wait_ge(sch.sems["DW"], 16))

    def dma(queue, out_ap, in_ap, sem, waits=()):
        sch.emit(queue, lambda e, o=out_ap, i=in_ap: e.dma_start(out=o, in_=i),
                 waits=waits, inc=sem, inc_n=16)
        return sch.val(sem)

    for dst, src in [(idbf, d_idbf), (idf, d_idf), (ones_r_bf, d_ones_r_bf),
                     (ones_r_f, d_ones_r_f), (ones_c_f, d_ones_c_f),
                     (sel, d_sel)]:
        dma("sync", dst[:], src[:], DW)
    consts_dw = dma("sync", a_sb[:], d_alog.rearrange("(dc p) s -> p dc s", p=128), DW)

    s0 = sch.emit("scalar", lambda e: e.activation(out=a_sb[:], in_=a_sb[:],
                  func=AF.Exp), waits=[(DW, consts_dw)], inc=S_)
    va = sch.emit("vector", lambda e: e.tensor_scalar(out=a_sb[:], in0=a_sb[:],
             scalar1=1e-6, scalar2=1.0, op0=ALU.max, op1=ALU.min),
             waits=[(S_, s0)], inc=V_)
    sch.emit("vector", lambda e, v=va: e.wait_ge(sch.sems["V"], v))
    a_ready = sch.emit("vector", lambda e: e.tensor_scalar(out=a_sb[:],
                       in0=a_sb[:], scalar1=-1.0, scalar2=None, op0=ALU.mult),
                       inc=V_)
    sch.emit("vector", lambda e: e.memset(H[:], 0.0), inc=V_)
    bdt_row = SB("bdt_row", [16, 128], F32)
    bdtw = dma("sync", bdt_row[:], d_bdt.rearrange("(k p) -> k p", p=128), DW)
    bdt_tp = sch.emit("tensor", lambda e: e.transpose(
        psTRf[:, 0:16], bdt_row[:], idf[0:16, 0:16]), waits=[(DW, bdtw)], inc=P_)
    bdt_done = sch.emit("vector", lambda e: e.tensor_copy(
        out=bdt_sb[:], in_=psTRf[:, 0:16]), waits=[(P_, bdt_tp)], inc=V_)
    trg["trf"] = ("V", bdt_done)
    sch.emit("vector", lambda e: e.memset(magic_col[:], MAGIC), inc=None)
    sch.emit("vector", lambda e: e.memset(eps_col[:], LN_EPS), inc=None)
    sch.emit("vector", lambda e: e.memset(one_col[:], 1.0), inc=None)
    sch.emit("vector", lambda e: e.memset(neghalf_col[:], -0.5), inc=None)
    sch.emit("vector", lambda e: e.memset(negmagic_col[:], -MAGIC), inc=V_)

    es_prep = ExitStack()
    nrow_f = SBe(es_prep, "nrow_f", [1, D], F32)
    nrow_b = SBe(es_prep, "nrow_b", [1, D], BF16)
    wbc_f = SBe(es_prep, "wbc_f", [128, 16, 16], F32)
    wtmp_f = [SBe(es_prep, f"wtmp_f{j}", [128, 4096], F32) for j in range(2)]
    wtmp_b = [SBe(es_prep, f"wtmp_b{j}", [128, 4096], BF16) for j in range(2)]
    sabs = SBe(es_prep, "sabs", [128, 2], F32)
    pr = SBe(es_prep, "pr", [128, 2, 512], BF16)
    sc8 = SBe(es_prep, "sc8", [1, 8], F32)

    rep_pm = 0
    for vec_d, rep in ([(d_nw, nw_rep), (d_nb, nb_rep)] if _PREP >= 1 else []):
        w = dma("sync", nrow_f[:], vec_d[None, :], DW,
                waits=[(V_, sch.val(V_))])
        cv = sch.emit("vector", lambda e: e.tensor_copy(out=nrow_b[:], in_=nrow_f[:]),
                      waits=[(DW, w), (P_, rep_pm)], inc=V_)
        for ns in range(2):
            j, ps, g = rot.next()
            rep_pm = sch.emit("tensor", lambda e, ps=ps, ns=ns: e.matmul(
                ps[:], ones_r_bf[:], nrow_b[0:1, ns * 512:(ns + 1) * 512],
                start=True, stop=True), waits=[(V_, cv)] + g, inc=P_)
            ev = sch.emit("vector", lambda e, ps=ps, rep=rep, ns=ns: e.tensor_copy(
                out=rep[:, ns * 512:(ns + 1) * 512], in_=ps[:]),
                waits=[(P_, rep_pm)], inc=V_)
            rot.set_guard(j, V_, ev)

    for col0, src in ([(0, d_wB), (16, d_wC)] if _PREP >= 1 else []):
        w = dma("sync", wbc_f[:], src.rearrange("(k p) s -> p k s", p=128), DW,
                waits=[(V_, sch.val(V_))])
        sch.emit("vector", lambda e, c0=col0: e.tensor_copy(
            out=wbc[:, :, c0:c0 + 16], in_=wbc_f[:]), waits=[(DW, w)], inc=V_)
    wbc_ready = sch.val(V_)

    prep_cast_guard = [("V", 0), ("V", 0)]

    def cast_weight(src, dst, rows, cols):
        nb = min(4096 // cols, rows // 128)
        segrows = nb * 128
        for sgi in range(rows // segrows):
            r0 = sgi * segrows
            h = sgi % 2
            wtf = wtmp_f[h][:, 0:nb * cols].rearrange("p (a b) -> p a b", a=nb)
            wtb = wtmp_b[h][:, 0:nb * cols].rearrange("p (a b) -> p a b", a=nb)
            w = dma("sync", wtf,
                    src[r0:r0 + segrows, :].rearrange("(k p) m -> p k m", p=128),
                    DW, waits=[prep_cast_guard[h]])
            cv = sch.emit("vector", lambda e, wtf=wtf, wtb=wtb: e.tensor_copy(
                out=wtb, in_=wtf), waits=[(DW, w)], inc=V_)
            prep_cast_guard[h] = (V_, cv)
            dma("sync",
                dst[r0:r0 + segrows, :].rearrange("(k p) m -> p k m", p=128),
                wtb, DW, waits=[(V_, cv)])

    if _PREP >= 2:
        cast_weight(d_win, s_win, D, DI)
        cast_weight(d_wgate, s_wgate, D, DI)
        cast_weight(d_wdt, s_wdt, DI, DI)
        cast_weight(d_wout, s_wout, DI, D)

    qt_state = {"pr_guard": [("V", 0), ("V", 0)], "prslot": [0],
                "tpg": [("P", 0), ("P", 0)]}

    def quant_transpose(src, dstT, rows, cols, s_bc, r_bc, si, p2=3):
        last_tp = sch.val(P_)
        nb = min(4096 // cols, rows // 128)
        segrows = nb * 128
        for sgi in range(rows // segrows):
            r0 = sgi * segrows
            h = sgi % 2
            wtf = wtmp_f[h][:, 0:nb * cols].rearrange("p (a b) -> p a b", a=nb)
            w = dma("sync", wtf,
                    src[r0:r0 + segrows, :].rearrange("(k p) m -> p k m", p=128),
                    DW, waits=[prep_cast_guard[h]])
            if sgi == 0:
                vr = sch.emit("vector", lambda e, wtf=wtf: e.reduce_sum(
                    out=sabs[:, 0:1], in_=wtf, axis=AX.XY,
                    apply_absolute_value=True), waits=[(DW, w)], inc=V_)
                prep_cast_guard[h] = (V_, vr)
            else:
                vr = sch.emit("vector", lambda e, wtf=wtf: e.reduce_sum(
                    out=sabs[:, 1:2], in_=wtf, axis=AX.XY,
                    apply_absolute_value=True), waits=[(DW, w)], inc=V_)
                prep_cast_guard[h] = (V_, vr)
                sch.emit("vector", lambda e, v=vr: e.wait_ge(sch.sems["V"], v))
                sch.emit("vector", lambda e: e.tensor_add(
                    out=sabs[:, 0:1], in0=sabs[:, 0:1], in1=sabs[:, 1:2]), inc=V_)
        pm = sch.emit("tensor", lambda e: e.transpose(
            psTRf[0:1, 0:128], sabs[:, 0:1], idf[:]),
            waits=[(V_, sch.val(V_)), trg["trf"]], inc=P_)
        sl = sc8[0:1, si:si + 1]
        sl2 = sc8[0:1, si + 1:si + 2]
        vr = sch.emit("vector", lambda e: e.reduce_sum(
            out=sabs[0:1, 1:2], in_=psTRf[0:1, 0:128], axis=AX.X),
            waits=[(P_, pm)], inc=V_)
        sch.emit("vector", lambda e, v=vr: e.wait_ge(sch.sems["V"], v))
        ev = sch.emit("vector", lambda e, sl=sl: e.tensor_scalar(
            out=sl, in0=sabs[0:1, 1:2], scalar1=1.0 / (rows * cols),
            scalar2=1e-5, op0=ALU.mult, op1=ALU.max), inc=V_)
        trg["trf"] = (V_, ev)
        sch.emit("vector", lambda e, v=ev: e.wait_ge(sch.sems["V"], v))
        sch.emit("vector", lambda e, sl=sl, sl2=sl2: e.reciprocal(
            out=sl2, in_=sl), inc=V_)
        vv = sch.val(V_)
        sch.emit("vector", lambda e, v=vv: e.wait_ge(sch.sems["V"], v))
        for val_ap, bc in [(sl, s_bc), (sl2, r_bc)]:
            j, ps, g = rot.next()
            pm = sch.emit("tensor", lambda e, ps=ps, v=val_ap: e.matmul(
                ps[:, 0:1], ones_r_f[:], v, start=True, stop=True),
                waits=[(V_, vv)] + g, inc=P_)
            ev = sch.emit("vector", lambda e, ps=ps, bc=bc: e.tensor_copy(
                out=bc[:], in_=ps[:, 0:1]), waits=[(P_, pm)], inc=V_)
            rot.set_guard(j, V_, ev)
        scales_v = sch.val(V_)
        if p2 == 0:
            return
        pr_guard = qt_state["pr_guard"]
        prslot = qt_state["prslot"]
        for sgi in range(rows // segrows):
            r0 = sgi * segrows
            h = sgi % 2
            wtf = wtmp_f[h][:, 0:nb * cols].rearrange("p (a b) -> p a b", a=nb)
            wtb = wtmp_b[h][:, 0:nb * cols].rearrange("p (a b) -> p a b", a=nb)
            w = dma("sync", wtf,
                    src[r0:r0 + segrows, :].rearrange("(k p) m -> p k m", p=128),
                    DW, waits=[prep_cast_guard[h]])
            sq = sch.emit("scalar", lambda e, wtf=wtf: e.activation(
                out=wtf, in_=wtf, func=AF.Identity,
                scale=r_bc[:, 0:1], bias=magic_col[:, 0:1]),
                waits=[(DW, w), (V_, scales_v)], inc=S_)
            sch.emit("vector", lambda e, wtf=wtf: e.tensor_scalar(
                out=wtf, in0=wtf, scalar1=-MAGIC,
                scalar2=-1.0, op0=ALU.add, op1=ALU.max),
                waits=[(S_, sq), qt_state["tpg"][h]], inc=None)
            cv = sch.emit("vector", lambda e, wtf=wtf, wtb=wtb: e.tensor_scalar(
                out=wtb, in0=wtf, scalar1=1.0,
                scalar2=None, op0=ALU.min), inc=V_)
            prep_cast_guard[h] = (V_, cv)
            if p2 == 1:
                continue
            for jb in range(cols // 128):
                for ibg in range((nb + 3) // 4):
                    nbl = min(4, nb - ibg * 4)
                    ltp = None
                    for ib2 in range(nbl):
                        ib = ibg * 4 + ib2
                        w = ([(V_, cv), trg["tr"]] if ib2 == 0
                             else [(P_, ltp)])
                        ltp = sch.emit("tensor", lambda e, ib=ib, ib2=ib2,
                                       jb=jb, wtb=wtb: e.transpose(
                            psTR[:, ib2 * 128:(ib2 + 1) * 128],
                            wtb[:, ib, jb * 128:(jb + 1) * 128], idbf[:]),
                            waits=w, inc=P_)
                    last_tp = ltp
                    qt_state["tpg"][h] = (P_, ltp)
                    slot = prslot[0] % 2
                    prslot[0] += 1
                    ev = sch.emit("vector", lambda e, slot=slot, nbl=nbl:
                                  e.tensor_copy(
                        out=pr[:, slot, 0:nbl * 128], in_=psTR[:, 0:nbl * 128]),
                        waits=[(P_, last_tp), pr_guard[slot]], inc=V_)
                    trg["tr"] = (V_, ev)
                    if p2 == 2:
                        pr_guard[slot] = (V_, ev)
                        continue
                    c0 = r0 + ibg * 512
                    st = dma("sync",
                             dstT[jb * 128:(jb + 1) * 128, c0:c0 + nbl * 128],
                             pr[:, slot, 0:nbl * 128], DW, waits=[(V_, ev)])
                    pr_guard[slot] = (DW, st)

    if _PREP >= 3:
        quant_transpose(d_fg, s_fgT, INNER, D, wgs_bc, wgr_bc, 0,
                        p2={3: 0, 4: 1, 5: 2, 6: 3}.get(_PREP, 3))
    if _PREP >= 7:
        quant_transpose(d_fu, s_fuT, INNER, D, wus_bc, wur_bc, 2)
        quant_transpose(d_fd, s_fdT, D, INNER, wds_bc, wdr_bc, 4)

    if "wq_probe" in dbg_outs and _PREP >= 6:
        w = dma("sync", pr[:, 0, :], s_fgT[0:128, 0:512], DW,
                waits=[(DW, sch.val(DW))])
        dma("sync", dbg_outs["wq_probe"][:], pr[:, 0, :], DX, waits=[(DW, w)])
    if "wq2_probe" in dbg_outs and _PREP >= 7:
        w = dma("sync", pr[:, 1, :], s_fdT[0:128, 0:512], DW,
                waits=[(DW, sch.val(DW))])
        dma("sync", dbg_outs["wq2_probe"][:], pr[:, 1, :], DX, waits=[(DW, w)])
    if "a_probe" in dbg_outs:
        dma("sync", dbg_outs["a_probe"][:], a_sb[:], DX, waits=[(V_, a_ready)])
    if "scales_probe" in dbg_outs and _PREP >= 3:
        pass
    if "scales_probe" in dbg_outs and _PREP >= 3:
        sp_b = SBe(es_prep, "sp_b", [128, 8], F32)
        for jj, bc in enumerate([wgs_bc, wgr_bc, wus_bc, wur_bc, wds_bc, wdr_bc]):
            sch.emit("vector", lambda e, jj=jj, bc=bc: e.tensor_copy(
                out=sp_b[:, jj:jj + 1], in_=bc[:]), inc=V_)
        dma("sync", dbg_outs["scales_probe"][:], sp_b[:], DX,
            waits=[(V_, sch.val(V_))])

    es_prep.close()
    prep_v = sch.val(V_)
    prep_s = sch.val(S_)
    prep_p = sch.val(P_)
    prep_dw = sch.val(DW)

    x_tmd = SB("x_tm", [128, 2, NTOK, D], F32)
    xn0_tmd = SB("xn0_tm", [128, 2, NTOK, D], BF16)
    tmA = SB("tmA", [128, 2, D], BF16)
    tmB = SB("tmB", [128, 2, D], BF16)
    xn_fm = SB("xn_fm", [128, 8, C], BF16)
    fmS = SB("fmS", [128, 8, C], BF16)
    gu_fm = SB("gu_fm", [128, 32, C], BF16)
    z_fm = gu_fm[:, 0:16, :]
    dt_fm = gu_fm[:, 16:32, :]
    gate_fm = SB("gate_fm", [128, 16, C], BF16)
    y_fm = SB("y_fm", [128, 16, C], BF16)
    f_fm = SB("f_fm", [128, 8, C], BF16)
    bc_sb = SB("bc_sb", [32, C], BF16)
    b_bcd = SB("b_bc", [128, 2, C], BF16)
    c_bcd = SB("c_bc", [128, 2, C], BF16)
    dAr = SB("dAr", [128, 2, C], BF16)
    br_s = SB("br", [128, C], BF16)
    hr = SB("hr", [128, 2, C], BF16)
    cmr2 = SB("cmr2", [128, 2, C], BF16)
    sp_r = SB("sp_r", [128, 1, C], BF16)
    f32r = [SB(f"f32r{j}", [128, C], F32) for j in range(2)]
    mhi = SB("mhi", [128, C], BF16)
    mlo = SB("mlo", [128, C], BF16)
    invsx_bc = SB("invsx_bc", [128, C], F32)
    comb_bc = SB("comb_bc", [128, C], F32)
    unsc_bc = SB("unsc_bc", [128, C], F32)
    row_f = SB("row_f", [1, 3 * C], BF16)
    stat = SB("stat", [128, 48], F32)
    statb = SB("statb", [128, 4], F32)
    sg_b = SB("sg_b", [128, C], BF16)

    seg_state = {"n": 0, "free": [("P", 0)] * NSLOT}

    def load_seg(dram_view):
        n = seg_state["n"]
        slot = n % NSLOT
        a = dram_view.shape[1]
        b = dram_view.shape[2]
        dst = wslot[slot][:, 0:a * b].rearrange("p (a b) -> p a b", a=a)
        w = dma("sync", dst, dram_view, DW,
                waits=[seg_state["free"][slot], (DW, prep_dw)])
        seg_state["n"] = n + 1
        return slot, w

    def seg_view(slot, a, b):
        return wslot[slot][:, 0:a * b].rearrange("p (a b) -> p a b", a=a)

    def seg_done(slot, sem, val):
        seg_state["free"][slot] = (sem, val)

    ln_pool = {"i": 0, "guard": [("V", 0)] * 4}

    def ln_norm(src_ap, out_ap, extra_waits=()):
        slot = ln_pool["i"] % 4
        ln_pool["i"] += 1
        c = 3 * slot
        bncol = stat[:, 24 + 12 * (slot % 2):24 + 12 * (slot % 2) + 12]
        mv = stat[:, c:c + 2]
        c2 = stat[:, c + 2:c + 3]
        bias = statb[:, slot:slot + 1]
        ew = list(extra_waits) + [ln_pool["guard"][slot]]
        sch.emit("vector", lambda e: e.bn_stats(
            out=bncol[:, 0:6], in_=src_ap[:, 0:512]), waits=ew, inc=V_)
        v1 = sch.emit("vector", lambda e: e.bn_stats(
            out=bncol[:, 6:12], in_=src_ap[:, 512:1024]), inc=V_)
        sch.emit("vector", lambda e: e.wait_ge(sch.sems["V"], v1))
        v2 = sch.emit("vector", lambda e: e.bn_aggr(
            out=mv, in_=bncol), inc=V_)
        sch.emit("vector", lambda e: e.wait_ge(sch.sems["V"], v2))
        v3 = sch.emit("vector", lambda e: e.tensor_scalar(
            out=mv[:, 0:1], in0=mv[:, 0:1], scalar1=-1.0, scalar2=None,
            op0=ALU.mult), inc=V_)
        s1 = sch.emit("scalar", lambda e: e.activation(
            out=c2, in_=mv[:, 1:2], func=AF.Ln, bias=eps_col[:, 0:1]),
            waits=[(V_, v2)], inc=S_)
        sch.emit("scalar", lambda e: e.wait_ge(sch.sems["S"], s1))
        s2 = sch.emit("scalar", lambda e: e.activation(
            out=c2, in_=c2, func=AF.Exp, scale=neghalf_col[:, 0:1]), inc=S_)
        sch.emit("scalar", lambda e: e.wait_ge(sch.sems["S"], s2))
        s3 = sch.emit("scalar", lambda e: e.activation(
            out=bias, in_=mv[:, 0:1], func=AF.Identity, scale=c2),
            waits=[(V_, v3)], inc=S_)
        sch.emit("scalar", lambda e: e.wait_ge(sch.sems["S"], s3))
        ap_s = sch.emit("scalar", lambda e: e.activation(
            out=out_ap, in_=src_ap, func=AF.Identity, scale=c2, bias=bias),
            inc=S_)
        ln_pool["guard"][slot] = (S_, ap_s)
        return ap_s

    def fm_mm(wsrc, rows, out_cols, rhs_fm, evac, rhs_ready, seg_mt=4,
              extra_mm_waits=()):
        K = rows // 128
        for sg in range(out_cols // (seg_mt * 128)):
            m0 = sg * seg_mt * 128
            slot, w = load_seg(wsrc[:, m0:m0 + seg_mt * 128].rearrange(
                "(k p) m -> p k m", p=128))
            sv = seg_view(slot, K, seg_mt * 128)
            lastmm = None
            for mt in range(seg_mt):
                j, ps, g = rot.next()
                for k in range(K):
                    waits = ([(DW, w), (V_, rhs_ready)] + g +
                             list(extra_mm_waits)) if k == 0 else ()
                    lastmm = sch.emit("tensor", lambda e, ps=ps, sv=sv, k=k,
                                      mt=mt, K=K: e.matmul(
                        ps[:], sv[:, k, mt * 128:(mt + 1) * 128],
                        rhs_fm[:, k, :], start=(k == 0), stop=(k == K - 1)),
                        waits=waits, inc=(P_ if k == K - 1 else None))
                sem, val = evac(sg * seg_mt + mt, ps, [(P_, lastmm)])
                rot.set_guard(j, sem, val)
            seg_done(slot, P_, lastmm)

    def transpose_tile(src_ap_fn, dst_fn, ready, n_jg=2, extra_first=(),
                       ev_sem="V"):
        for jg in range(n_jg):
            tp = None
            for jj in range(4):
                j = jg * 4 + jj
                w = ([ready, trg["tr"]] + list(extra_first)) if jj == 0 \
                    else [(P_, tp)]
                tp = sch.emit("tensor", lambda e, j=j, jj=jj, jg=jg: e.transpose(
                    psTR[:, jj * 128 + jg * 512:(jj + 1) * 128 + jg * 512],
                    src_ap_fn(j), idbf[:]), waits=w, inc=P_)
            ev = dst_fn(jg, [(P_, tp)])
            trg["tr"] = (ev_sem, ev)

    out_dx = 0

    st = [dict() for _ in range(NCH)]
    gd = {
        "xpar": [[("V", 0)], [("V", 0)]],
        "x0par": [[("V", 0)], [("V", 0)]],
        "xnfm": [("V", 0)],
        "zdt": [("V", 0)],
        "fmS": [("V", 0)],
        "xstore": [0, 0],
    }
    rngT = {"i": 0, "g": [("V", 0), ("V", 0)]}
    rngB = {"i": 0, "g": [("V", 0), ("V", 0)]}

    def phase_A(ch):
        t0 = ch * C
        par = ch % 2
        x_tm = x_tmd[:, par]
        xn0_tm = xn0_tmd[:, par]
        fw = list(gd["xpar"][par]) + [(DX, gd["xstore"][par])]
        if ch == 0:
            fw += [(V_, prep_v), (S_, prep_s), (P_, prep_p), (DW, prep_dw)]
        xw = dma("scalar", x_tm[:],
                 dx[t0:t0 + C, :].rearrange("(i p) m -> p i m", p=128), DX,
                 waits=fw)
        for i in range(NTOK):
            s1 = ln_norm(x_tm[:, i, :], xn0_tm[:, i, :],
                         extra_waits=[(DX, xw)] + gd["x0par"][par])
            sl = rngT["i"] % 2
            rngT["i"] += 1
            xs = tmA[:, sl, :]
            s2 = ln_norm(xn0_tm[:, i, :], xs, extra_waits=[(S_, s1),
                                                           rngT["g"][sl]])
            sch.emit("vector", lambda e, xs=xs: e.tensor_mul(
                out=xs, in0=xs, in1=nw_rep[:]), waits=[(S_, s2)], inc=None)
            xnv = sch.emit("vector", lambda e, xs=xs: e.tensor_add(
                out=xs, in0=xs, in1=nb_rep[:]), inc=V_)

            def dst_fn(jg, waits, i=i):
                return sch.emit("scalar", lambda e, i=i, jg=jg: e.activation(
                    out=xn_fm[:, jg * 4:(jg + 1) * 4, i * 128:(i + 1) * 128],
                    in_=psTR[:, jg * 512:(jg + 1) * 512].rearrange(
                        "p (a b) -> p a b", a=4), func=AF.Identity),
                    waits=waits, inc=S_)
            tp0 = sch.val(P_)
            transpose_tile(lambda j, xs=xs: xs[:, j * 128:(j + 1) * 128],
                           dst_fn, (V_, xnv), extra_first=gd["xnfm"],
                           ev_sem="S")
            rngT["g"][sl] = (P_, sch.val(P_))
        xnr = sch.emit("vector", lambda e: e.nop(),
                       waits=[(S_, sch.val(S_))], inc=V_)
        st[ch]["xn_ready"] = xnr
        st[ch]["xw"] = xw

    def phase_B(ch):
        xn_ready = st[ch]["xn_ready"]
        zdt_guard = gd["zdt"]

        def z_evac(mt, ps, waits):
            v = sch.emit("scalar", lambda e, mt=mt, ps=ps: e.activation(
                out=z_fm[:, mt, :], in_=ps[:], func=AF.Identity),
                waits=waits + zdt_guard, inc=S_)
            return S_, v
        fm_mm(s_win, D, DI, xn_fm, z_evac, xn_ready, seg_mt=4)
        z_ready_s = sch.val(S_)
        z_ready = sch.emit("vector", lambda e: e.nop(),
                           waits=[(S_, z_ready_s)], inc=V_)

        j, ps, g = rot.next()
        lastmm = None
        for k in range(16):
            waits = ([(V_, z_ready), (V_, wbc_ready)] + g) if k == 0 else ()
            lastmm = sch.emit("tensor", lambda e, ps=ps, k=k: e.matmul(
                ps[0:32, :], wbc[:, k, :], z_fm[:, k, :],
                start=(k == 0), stop=(k == 15)),
                waits=waits, inc=(P_ if k == 15 else None))
        bc_ready = sch.emit("vector", lambda e, ps=ps: e.tensor_copy(
            out=bc_sb[:], in_=ps[0:32, :]), waits=[(P_, lastmm)], inc=V_)
        rot.set_guard(j, V_, bc_ready)

        sp_guard = {0: ("V", 0)}
        dt_rdy = [0] * 16

        def dt_evac(mt, ps, waits):
            s1 = sch.emit("scalar", lambda e, ps=ps, mt=mt: e.activation(
                out=sp_r[:, 0, :], in_=ps[:], func=AF.Exp,
                bias=bdt_sb[:, mt:mt + 1]), waits=waits + [sp_guard[0]], inc=S_)
            s2 = sch.emit("scalar", lambda e: e.activation(
                out=sp_r[:, 0, :], in_=sp_r[:, 0, :], func=AF.Ln,
                bias=one_col[:, 0:1]), inc=S_)
            v = sch.emit("vector", lambda e, mt=mt: e.tensor_scalar(
                out=dt_fm[:, mt, :], in0=sp_r[:, 0, :], scalar1=0.001,
                scalar2=1.0, op0=ALU.max, op1=ALU.min),
                waits=[(S_, s2)] + zdt_guard, inc=V_)
            sp_guard[0] = (V_, v)
            dt_rdy[mt] = v
            return S_, s1
        fm_mm(s_wdt, DI, DI, z_fm, dt_evac, z_ready, seg_mt=2)
        dt_ready = sch.val(V_)
        st[ch]["dt_rdy"] = dt_rdy

        def gate_evac(mt, ps, waits):
            s = sch.emit("scalar", lambda e, ps=ps, mt=mt: e.activation(
                out=gate_fm[:, mt, :], in_=ps[:], func=AF.Silu),
                waits=waits, inc=S_)
            return S_, s
        fm_mm(s_wgate, D, DI, xn_fm, gate_evac, xn_ready, seg_mt=4)
        gate_ready_s = sch.val(S_)
        gd["xnfm"] = [(P_, sch.val(P_))]

        st[ch].update(z_ready=z_ready, dt_ready=dt_ready, bc_ready=bc_ready,
                      gate_ready_s=gate_ready_s)
        if ch == 0 and "z_fm" in dbg_outs:
            dma("sync", dbg_outs["z_fm"][:], z_fm, DX, waits=[(V_, z_ready)])
        if ch == 0 and "dt_fm" in dbg_outs:
            dma("sync", dbg_outs["dt_fm"][:], dt_fm, DX, waits=[(V_, dt_ready)])
        if ch == 0 and "gate_fm" in dbg_outs:
            dma("sync", dbg_outs["gate_fm"][:], gate_fm[:], DX,
                waits=[(S_, gate_ready_s)])
        if ch == 0 and "bc_sb" in dbg_outs:
            dma("sync", dbg_outs["bc_sb"][:], bc_sb[:], DX,
                waits=[(V_, bc_ready)])
        if ch == 0 and "xn_fm" in dbg_outs:
            dma("sync", dbg_outs["xn_fm"][:], xn_fm[:], DX,
                waits=[(V_, xn_ready)])

    def phase_scan(ch):
        dt_rdy = st[ch]["dt_rdy"]
        bc_ready = st[ch]["bc_ready"]
        u_rdy = [0] * 16
        for dc in range(16):
            u_rdy[dc] = sch.emit("vector", lambda e, dc=dc: e.tensor_mul(
                out=z_fm[:, dc, :], in0=z_fm[:, dc, :], in1=dt_fm[:, dc, :]),
                waits=[(V_, dt_rdy[dc])], inc=V_)

        NGP = 0
        pend_carry = []
        cm_guard = [("V", 0), ("V", 0)]
        ring_g = [[("V", 0)], [("V", 0)]]
        sweep_hv = [0, 0]
        for s in range(16):
            bpar = s % 2
            b_bc = b_bcd[:, bpar, :]
            c_bc = c_bcd[:, bpar, :]
            prev_sweep = [(V_, sweep_hv[bpar]), (G_, sch.val(G_))]
            j, ps, g = rot.next()
            pm = sch.emit("tensor", lambda e, ps=ps, s=s: e.matmul(
                ps[:], sel[:, s * 128:(s + 1) * 128], bc_sb[:],
                start=True, stop=True), waits=[(V_, bc_ready)] + g, inc=P_)
            bv = sch.emit("scalar", lambda e, ps=ps, b_bc=b_bc: e.activation(
                out=b_bc, in_=ps[:], func=AF.Identity),
                waits=[(P_, pm)] + prev_sweep, inc=S_)
            rot.set_guard(j, S_, bv)
            j, ps, g = rot.next()
            pm = sch.emit("tensor", lambda e, ps=ps, s=s: e.matmul(
                ps[:], sel[:, (16 + s) * 128:(17 + s) * 128], bc_sb[:],
                start=True, stop=True), waits=g, inc=P_)
            cv = sch.emit("scalar", lambda e, ps=ps, c_bc=c_bc: e.activation(
                out=c_bc, in_=ps[:], func=AF.Identity),
                waits=[(P_, pm)] + prev_sweep, inc=S_)
            rot.set_guard(j, S_, cv)

            for dc in range(16):
                it = s * 16 + dc
                sl = it % 2
                col = s * 16 + dc
                sa = sch.emit("scalar", lambda e, sl=sl, dc=dc, s=s: e.activation(
                    out=dAr[:, sl, :], in_=dt_fm[:, dc, :], func=AF.Exp,
                    scale=a_sb[:, dc, s:s + 1]),
                    waits=ring_g[sl] + [(V_, dt_rdy[dc]), (V_, a_ready)], inc=S_)
                if len(pend_carry) >= 2:
                    psl, pcol = pend_carry.pop(0)
                    sch.emit("vector", lambda e, psl=psl, pcol=pcol:
                             e.tensor_copy(out=H[:, pcol:pcol + 1],
                                           in_=hr[:, psl, C - 1:C]), inc=None)
                sch.emit("vector", lambda e, sl=sl, dc=dc, b_bc=b_bc:
                         e.tensor_mul(
                    out=br_s[:], in0=z_fm[:, dc, :], in1=b_bc),
                    waits=[(V_, u_rdy[dc]), (S_, bv)], inc=None)
                hv = sch.emit("vector", lambda e, sl=sl, col=col:
                              e.tensor_tensor_scan(
                    out=hr[:, sl, :], data0=dAr[:, sl, :], data1=br_s[:],
                    initial=H[:, col:col + 1], op0=ALU.mult, op1=ALU.add),
                    waits=[(S_, sa)], inc=V_)
                pend_carry.append((sl, col))
                if dc < NGP:
                    cm = sch.emit("vector", lambda e, sl=sl, c_bc=c_bc:
                                  e.tensor_mul(
                        out=cmr2[:, sl, :], in0=hr[:, sl, :], in1=c_bc),
                        waits=[(S_, cv), cm_guard[sl]], inc=V_)
                    if s == 0:
                        gy = sch.emit("gpsimd", lambda e, sl=sl, dc=dc:
                                      e.tensor_copy(
                            out=y_fm[:, dc, :], in_=cmr2[:, sl, :]),
                            waits=[(V_, cm)], inc=G_)
                    else:
                        gy = sch.emit("gpsimd", lambda e, sl=sl, dc=dc:
                                      e.tensor_add(
                            out=y_fm[:, dc, :], in0=y_fm[:, dc, :],
                            in1=cmr2[:, sl, :]), waits=[(V_, cm)], inc=G_)
                    cm_guard[sl] = (G_, gy)
                else:
                    if s == 0:
                        sch.emit("vector", lambda e, sl=sl, dc=dc, c_bc=c_bc:
                                 e.tensor_mul(
                            out=y_fm[:, dc, :], in0=hr[:, sl, :], in1=c_bc),
                            waits=[(S_, cv)], inc=None)
                    else:
                        sch.emit("vector", lambda e, sl=sl, c_bc=c_bc:
                                 e.tensor_mul(
                            out=br_s[:], in0=hr[:, sl, :], in1=c_bc),
                            waits=[(S_, cv)], inc=None)
                        sch.emit("vector", lambda e, dc=dc: e.tensor_add(
                            out=y_fm[:, dc, :], in0=y_fm[:, dc, :],
                            in1=br_s[:]), inc=None)
                ring_g[sl] = [(V_, hv)]
            sweep_hv[bpar] = sch.val(V_)
        while pend_carry:
            psl, pcol = pend_carry.pop()
            vv_ = sch.val(V_)
            sch.emit("vector", lambda e, v=vv_: e.wait_ge(sch.sems["V"], v))
            sch.emit("vector", lambda e, psl=psl, pcol=pcol: e.tensor_copy(
                out=H[:, pcol:pcol + 1], in_=hr[:, psl, C - 1:C]), inc=None)
        y_ready = sch.emit("vector", lambda e: e.nop(),
                           waits=[(G_, sch.val(G_))], inc=V_)
        st[ch]["y_ready"] = y_ready
        if ch == 0 and "y_fm" in dbg_outs:
            dma("sync", dbg_outs["y_fm"][:], y_fm[:], DX, waits=[(V_, y_ready)])

    def phase_D(ch):
        par = ch % 2
        x_tm = x_tmd[:, par]
        xn0_tm = xn0_tmd[:, par]
        y_ready = st[ch]["y_ready"]
        gate_ready_s = st[ch]["gate_ready_s"]
        for k in range(16):
            sch.emit("vector", lambda e, k=k: e.tensor_mul(
                out=gate_fm[:, k, :], in0=y_fm[:, k, :], in1=gate_fm[:, k, :]),
                waits=([(S_, gate_ready_s), (V_, y_ready)] if k == 0 else ()),
                inc=(V_ if k == 15 else None))
        yg_ready = sch.val(V_)

        def attn_evac(mt, ps, waits):
            v = sch.emit("scalar", lambda e, mt=mt, ps=ps: e.activation(
                out=fmS[:, mt, :], in_=ps[:], func=AF.Identity),
                waits=waits, inc=S_)
            return S_, v
        fm_mm(s_wout, DI, D, gate_fm, attn_evac, yg_ready, seg_mt=2)
        attn_ready_s = sch.val(S_)
        attn_ready = sch.emit("vector", lambda e: e.nop(),
                              waits=[(S_, attn_ready_s)], inc=V_)
        if ch == 0 and "attn_fm" in dbg_outs:
            dma("sync", dbg_outs["attn_fm"][:], fmS[:], DX,
                waits=[(V_, attn_ready)])

        x1v = 0
        for i in range(NTOK):
            sl = rngB["i"] % 2
            rngB["i"] += 1
            rb = tmB[:, sl, :]

            def dst_fn(jg, waits, i=i, rb=rb):
                return sch.emit("vector", lambda e, i=i, jg=jg, rb=rb:
                                e.tensor_add(
                    out=rb[:, jg * 512:(jg + 1) * 512].rearrange(
                        "p (a b) -> p a b", a=4),
                    in0=psTR[:, jg * 512:(jg + 1) * 512].rearrange(
                        "p (a b) -> p a b", a=4),
                    in1=xn0_tm[:, i, jg * 512:(jg + 1) * 512].rearrange(
                        "p (a b) -> p a b", a=4)),
                    waits=waits + ([rngB["g"][sl]] if jg == 0 else []), inc=V_)
            transpose_tile(lambda j, i=i: fmS[:, j, i * 128:(i + 1) * 128],
                           dst_fn, (V_, attn_ready))
            sl2 = rngT["i"] % 2
            rngT["i"] += 1
            ra = tmA[:, sl2, :]
            s3 = ln_norm(rb, ra, extra_waits=[(V_, sch.val(V_)),
                                              rngT["g"][sl2]])
            rngB["g"][sl] = (S_, s3)
            x1v = sch.emit("vector", lambda e, i=i, ra=ra: e.tensor_add(
                out=x_tm[:, i, :], in0=x_tm[:, i, :], in1=ra),
                waits=[(S_, s3)], inc=V_)
            rngT["g"][sl2] = (V_, x1v)
        st[ch]["x1_ready"] = x1v
        gd["x0par"][par] = [(V_, x1v)]
        if ch == 0 and "x1_tm" in dbg_outs:
            dma("sync", dbg_outs["x1_tm"][:], x_tm[:], DX, waits=[(V_, x1v)])

    def phase_E(ch):
        par = ch % 2
        x_tm = x_tmd[:, par]
        x1_ready = st[ch]["x1_ready"]
        xqv = 0
        for i in range(NTOK):
            sl = rngB["i"] % 2
            rngB["i"] += 1
            rb = tmB[:, sl, :]
            s4 = ln_norm(x_tm[:, i, :], rb,
                         extra_waits=[(V_, x1_ready), rngB["g"][sl]])
            mx = stat[:, 15:16]
            sc4 = stat[:, 16:17]
            vr = sch.emit("vector", lambda e, rb=rb, mx=mx: e.reduce_max(
                out=mx, in_=rb, axis=AX.X, apply_absolute_value=True),
                waits=[(S_, s4)], inc=V_)
            sch.emit("vector", lambda e, v=vr: e.wait_ge(sch.sems["V"], v))
            vr = sch.emit("vector", lambda e, mx=mx, sc4=sc4: e.reciprocal(
                out=sc4, in_=mx), inc=V_)
            sch.emit("vector", lambda e, v=vr: e.wait_ge(sch.sems["V"], v))
            vv = sch.emit("vector", lambda e, sc4=sc4: e.tensor_scalar(
                out=sc4, in0=sc4, scalar1=127.0, scalar2=None, op0=ALU.mult),
                inc=V_)
            sch.emit("vector", lambda e, mx=mx, i=i: e.tensor_scalar(
                out=stat[:, 17 + i:18 + i], in0=mx, scalar1=1.0 / 127.0,
                scalar2=None, op0=ALU.mult), inc=None)
            s5 = sch.emit("scalar", lambda e, rb=rb, sc4=sc4: e.activation(
                out=f32r[0][:], in_=rb[:, 0:512], func=AF.Identity,
                scale=sc4, bias=magic_col[:, 0:1]), waits=[(V_, vv)], inc=S_)
            s6 = sch.emit("scalar", lambda e, rb=rb, sc4=sc4: e.activation(
                out=f32r[1][:], in_=rb[:, 512:1024], func=AF.Identity,
                scale=sc4, bias=magic_col[:, 0:1]), inc=S_)
            sch.emit("vector", lambda e, rb=rb: e.tensor_scalar(
                out=rb[:, 0:512], in0=f32r[0][:], scalar1=-MAGIC,
                scalar2=None, op0=ALU.add), waits=[(S_, s5)], inc=None)
            xqv = sch.emit("vector", lambda e, rb=rb: e.tensor_scalar(
                out=rb[:, 512:1024], in0=f32r[1][:], scalar1=-MAGIC,
                scalar2=None, op0=ALU.add), waits=[(S_, s6)], inc=V_)
            tp = sch.emit("tensor", lambda e, i=i: e.transpose(
                psTRf[0:1, 0:128], stat[:, 17 + i:18 + i], idf[:]),
                waits=[(V_, xqv), trg["trf"]], inc=P_)
            rv = sch.emit("vector", lambda e, i=i: e.tensor_copy(
                out=row_f[0:1, i * 128:(i + 1) * 128], in_=psTRf[0:1, 0:128]),
                waits=[(P_, tp)], inc=V_)
            trg["trf"] = (V_, rv)

            def dst_fn(jg, waits, i=i):
                return sch.emit("scalar", lambda e, i=i, jg=jg: e.activation(
                    out=fmS[:, jg * 4:(jg + 1) * 4, i * 128:(i + 1) * 128],
                    in_=psTR[:, jg * 512:(jg + 1) * 512].rearrange(
                        "p (a b) -> p a b", a=4), func=AF.Identity),
                    waits=waits, inc=S_)
            transpose_tile(lambda j, rb=rb: rb[:, j * 128:(j + 1) * 128],
                           dst_fn, (V_, xqv), ev_sem="S")
            rngB["g"][sl] = (P_, sch.val(P_))
        xhat_ready = sch.emit("vector", lambda e: e.nop(),
                              waits=[(S_, sch.val(S_))], inc=V_)
        invsx_row_ready = sch.val(V_)
        if ch == 0 and "xhat_fm" in dbg_outs:
            dma("sync", dbg_outs["xhat_fm"][:], fmS[:], DX,
                waits=[(V_, xhat_ready)])

        j, ps, g = rot.next()
        pm = sch.emit("tensor", lambda e, ps=ps: e.matmul(
            ps[:], ones_r_bf[:], row_f[0:1, 0:C], start=True, stop=True),
            waits=[(V_, invsx_row_ready)] + g, inc=P_)
        invsx_ready = sch.emit("vector", lambda e, ps=ps: e.tensor_copy(
            out=invsx_bc[:], in_=ps[:]), waits=[(P_, pm)], inc=V_)
        rot.set_guard(j, V_, invsx_ready)

        sch.emit("vector", lambda e: e.memset(mhi[:], -1e30), inc=None)
        mm0 = sch.emit("vector", lambda e: e.memset(mlo[:], 1e30), inc=V_)

        gu_state = {"sgs": ("S", 0)}
        for q in range(8):
            m0 = q * 512
            slot_g, wg = load_seg(s_fgT[:, m0:m0 + 512].rearrange(
                "(k p) m -> p k m", p=128))
            slot_u, wu = load_seg(s_fuT[:, m0:m0 + 512].rearrange(
                "(k p) m -> p k m", p=128))
            svg = seg_view(slot_g, 8, 512)
            svu = seg_view(slot_u, 8, 512)
            glast = ulast = None
            for mt in range(4):
                gm = q * 4 + mt
                jg_, psg, gg = rot.next()
                for k in range(8):
                    waits = ([(DW, wg), (V_, xhat_ready)] + gg) if k == 0 else ()
                    glast = sch.emit("tensor", lambda e, ps=psg, sv=svg, k=k,
                                     mt=mt: e.matmul(
                        ps[:], sv[:, k, mt * 128:(mt + 1) * 128], fmS[:, k, :],
                        start=(k == 0), stop=(k == 7)),
                        waits=waits, inc=(P_ if k == 7 else None))
                ju_, psu, gu_ = rot.next()
                for k in range(8):
                    waits = ([(DW, wu)] + gu_) if k == 0 else ()
                    ulast = sch.emit("tensor", lambda e, ps=psu, sv=svu, k=k,
                                     mt=mt: e.matmul(
                        ps[:], sv[:, k, mt * 128:(mt + 1) * 128], fmS[:, k, :],
                        start=(k == 0), stop=(k == 7)),
                        waits=waits, inc=(P_ if k == 7 else None))
                v1 = sch.emit("vector", lambda e, ps=psg: e.tensor_mul(
                    out=f32r[0][:], in0=ps[:], in1=invsx_bc[:]),
                    waits=[(P_, glast), (V_, invsx_ready), (V_, mm0),
                           gu_state["sgs"]], inc=V_)
                rot.set_guard(jg_, V_, v1)
                sgs = sch.emit("scalar", lambda e: e.activation(
                    out=sg_b[:], in_=f32r[0][:], func=AF.Silu,
                    scale=wgs_bc[:, 0:1]), waits=[(V_, v1)], inc=S_)
                gu_state["sgs"] = (S_, sgs)
                v2 = sch.emit("vector", lambda e, ps=psu: e.tensor_mul(
                    out=f32r[1][:], in0=ps[:], in1=invsx_bc[:]),
                    waits=[(P_, ulast)], inc=V_)
                rot.set_guard(ju_, V_, v2)
                sch.emit("vector", lambda e, gm=gm: e.tensor_mul(
                    out=gu_fm[:, gm, :], in0=f32r[1][:], in1=sg_b[:]),
                    waits=[(S_, sgs)], inc=None)
                sch.emit("vector", lambda e, gm=gm: e.tensor_max(
                    out=mhi[:], in0=mhi[:], in1=gu_fm[:, gm, :]), inc=None)
                sch.emit("vector", lambda e, gm=gm: e.tensor_tensor(
                    out=mlo[:], in0=mlo[:], in1=gu_fm[:, gm, :], op=ALU.min),
                    inc=(V_ if mt == 3 else None))
            seg_done(slot_g, P_, glast)
            seg_done(slot_u, P_, ulast)
        gu_ready = sch.val(V_)
        if ch == 0 and "gu_fm" in dbg_outs:
            dma("sync", dbg_outs["gu_fm"][:], gu_fm[:], DX,
                waits=[(V_, gu_ready)])

        sch.emit("vector", lambda e: e.tensor_scalar(
            out=mlo[:], in0=mlo[:], scalar1=-1.0, scalar2=None, op0=ALU.mult),
            inc=None)
        mab = sch.emit("vector", lambda e: e.tensor_max(
            out=mhi[:], in0=mhi[:], in1=mlo[:]), inc=V_)

        mtp = 0
        for i in range(NTOK):
            tp = sch.emit("tensor", lambda e, i=i: e.transpose(
                psTR[:, 0:128], mhi[:, i * 128:(i + 1) * 128], idbf[:]),
                waits=[(V_, mab), trg["tr"]], inc=P_)
            ev = sch.emit("vector", lambda e: e.tensor_copy(
                out=sg_b[:, 0:128], in_=psTR[:, 0:128]),
                waits=[(P_, tp), (P_, mtp)], inc=V_)
            trg["tr"] = (V_, ev)

            def vstep(fn, waits=()):
                v = sch.emit("vector", fn, waits=waits, inc=V_)
                sch.emit("vector", lambda e, v=v: e.wait_ge(sch.sems["V"], v))
                return v
            vstep(lambda e: e.reduce_max(
                out=stat[:, 21:22], in_=sg_b[:, 0:128], axis=AX.X))
            vstep(lambda e: e.reciprocal(
                out=stat[:, 22:23], in_=stat[:, 21:22]))
            vstep(lambda e: e.tensor_scalar(
                out=stat[:, 22:23], in0=stat[:, 22:23], scalar1=127.0,
                scalar2=None, op0=ALU.mult))
            vstep(lambda e: e.tensor_mul(
                out=stat[:, 21:22], in0=stat[:, 21:22], in1=wus_bc[:]))
            vstep(lambda e: e.tensor_mul(
                out=stat[:, 21:22], in0=stat[:, 21:22], in1=wds_bc[:]))
            cu = sch.emit("vector", lambda e: e.tensor_scalar(
                out=stat[:, 21:22], in0=stat[:, 21:22], scalar1=1.0 / 127.0,
                scalar2=None, op0=ALU.mult), inc=V_)
            for cidx, off in [(22, C), (21, 2 * C)]:
                tp = sch.emit("tensor", lambda e, cidx=cidx: e.transpose(
                    psTRf[0:1, 0:128], stat[:, cidx:cidx + 1], idf[:]),
                    waits=[(V_, cu), trg["trf"]], inc=P_)
                rv = sch.emit("vector", lambda e, i=i, off=off: e.tensor_copy(
                    out=row_f[0:1, off + i * 128:off + (i + 1) * 128],
                    in_=psTRf[0:1, 0:128]), waits=[(P_, tp)], inc=V_)
                trg["trf"] = (V_, rv)
            mtp = sch.val(P_)
        rows_ready = sch.val(V_)

        for dst, off in [(comb_bc, C), (unsc_bc, 2 * C)]:
            j, ps, g = rot.next()
            pm = sch.emit("tensor", lambda e, ps=ps, off=off: e.matmul(
                ps[:], ones_r_bf[:], row_f[0:1, off:off + C], start=True,
                stop=True), waits=[(V_, rows_ready)] + g, inc=P_)
            ev = sch.emit("vector", lambda e, ps=ps, dst=dst: e.tensor_copy(
                out=dst[:], in_=ps[:]), waits=[(P_, pm)], inc=V_)
            rot.set_guard(j, V_, ev)
        comb_ready = sch.val(V_)

        f32g = [("S", 0), ("S", 0)]
        for gm in range(32):
            v1 = sch.emit("vector", lambda e, gm=gm: e.tensor_mul(
                out=f32r[gm % 2][:], in0=gu_fm[:, gm, :], in1=comb_bc[:]),
                waits=([(V_, comb_ready)] if gm < 2 else []) + [f32g[gm % 2]],
                inc=V_)
            sq = sch.emit("scalar", lambda e, gm=gm: e.activation(
                out=f32r[gm % 2][:], in_=f32r[gm % 2][:], func=AF.Identity,
                bias=magic_col[:, 0:1]), waits=[(V_, v1)], inc=S_)
            sch.emit("scalar", lambda e, v=sq: e.wait_ge(sch.sems["S"], v))
            sb2 = sch.emit("scalar", lambda e, gm=gm: e.activation(
                out=gu_fm[:, gm, :], in_=f32r[gm % 2][:], func=AF.Identity,
                bias=negmagic_col[:, 0:1]), inc=S_)
            f32g[gm % 2] = (S_, sb2)
        x2h_s = sch.val(S_)
        x2h_ready = sch.emit("vector", lambda e: e.nop(),
                             waits=[(S_, x2h_s)], inc=V_)
        if ch == 0 and "x2hat_fm" in dbg_outs:
            dma("sync", dbg_outs["x2hat_fm"][:], gu_fm[:], DX,
                waits=[(V_, x2h_ready)])

        def down_evac(mt, ps, waits):
            v = sch.emit("vector", lambda e, mt=mt, ps=ps: e.tensor_mul(
                out=f_fm[:, mt, :], in0=ps[:], in1=unsc_bc[:]),
                waits=waits, inc=V_)
            return V_, v
        fm_mm(s_fdT, INNER, D, gu_fm, down_evac, x2h_ready, seg_mt=1,
              extra_mm_waits=[(V_, comb_ready)])
        f_ready = sch.val(V_)
        gd["zdt"] = [(P_, sch.val(P_))]
        if ch == 0 and "f_fm" in dbg_outs:
            dma("sync", dbg_outs["f_fm"][:], f_fm[:], DX, waits=[(V_, f_ready)])

        t0 = ch * C
        xv = 0
        for i in range(NTOK):
            sl = rngB["i"] % 2
            rngB["i"] += 1
            rb = tmB[:, sl, :]

            def dst_fn(jg, waits, i=i, rb=rb):
                return sch.emit("vector", lambda e, jg=jg, rb=rb: e.tensor_copy(
                    out=rb[:, jg * 512:(jg + 1) * 512],
                    in_=psTR[:, jg * 512:(jg + 1) * 512]),
                    waits=waits + ([rngB["g"][sl]] if jg == 0 else []), inc=V_)
            transpose_tile(lambda j, i=i: f_fm[:, j, i * 128:(i + 1) * 128],
                           dst_fn, (V_, f_ready))
            sl2 = rngT["i"] % 2
            rngT["i"] += 1
            ra = tmA[:, sl2, :]
            s7 = ln_norm(rb, ra, extra_waits=[(V_, sch.val(V_)),
                                              rngT["g"][sl2]])
            rngB["g"][sl] = (S_, s7)
            xv = sch.emit("vector", lambda e, i=i, ra=ra: e.tensor_add(
                out=x_tm[:, i, :], in0=x_tm[:, i, :], in1=ra),
                waits=[(S_, s7)], inc=V_)
            rngT["g"][sl2] = (V_, xv)
            gd["xstore"][par] = dma(
                "scalar", d_out[t0 + i * 128:t0 + (i + 1) * 128, :],
                x_tm[:, i, :], DX, waits=[(V_, xv)])
        gd["xpar"][par] = [(V_, xv)]

    nch_run = NCH if _STAGE >= 7 else (0 if _STAGE < 2 else 1)
    for ch in range(nch_run):
        phase_A(ch)
        if ch > 0 and _STAGE >= 5:
            phase_D(ch - 1)
            if _STAGE >= 6:
                phase_E(ch - 1)
        if _STAGE >= 3:
            phase_B(ch)
        if _STAGE >= 4:
            phase_scan(ch)
        if ch == 0 and "xn0_tm" in dbg_outs:
            dma("sync", dbg_outs["xn0_tm"][:], xn0_tmd[:, 0], DX,
                waits=[(V_, st[0]["xn_ready"]), (DW, sch.val(DW))])
    if nch_run and _STAGE >= 5:
        phase_D(nch_run - 1)
        if _STAGE >= 6:
            phase_E(nch_run - 1)

    sch.emit("sync", lambda e: None, waits=[(DX, sch.val(DX)),
                                            (DW, sch.val(DW))])
    sch.realize(block)
    es.close()
    return nc, sorted(dbg_outs.keys())


def kernel(L0_attn_x, L0_attn_norm_w, L0_attn_norm_b, L0_attn_W_in,
           L0_attn_W_gate, L0_attn_W_dt, L0_attn_b_dt, L0_attn_W_B,
           L0_attn_W_C, L0_attn_A_log, L0_attn_W_out, L0_ffn_w_gate,
           L0_ffn_w_up, L0_ffn_w_down):
    nc, dbg_names = build_program()

    sel = np.zeros((32, 32 * 128), dtype=BFNP)
    for s in range(32):
        sel[s, s * 128:(s + 1) * 128] = 1

    common = {
        "norm_w": np.ascontiguousarray(L0_attn_norm_w),
        "norm_b": np.ascontiguousarray(L0_attn_norm_b),
        "W_in": np.ascontiguousarray(L0_attn_W_in),
        "W_gate": np.ascontiguousarray(L0_attn_W_gate),
        "W_dt": np.ascontiguousarray(L0_attn_W_dt),
        "b_dt": np.ascontiguousarray(L0_attn_b_dt),
        "W_B": np.ascontiguousarray(L0_attn_W_B),
        "W_C": np.ascontiguousarray(L0_attn_W_C),
        "A_log": np.ascontiguousarray(L0_attn_A_log),
        "W_out": np.ascontiguousarray(L0_attn_W_out),
        "ffn_w_gate": np.ascontiguousarray(L0_ffn_w_gate),
        "ffn_w_up": np.ascontiguousarray(L0_ffn_w_up),
        "ffn_w_down": np.ascontiguousarray(L0_ffn_w_down),
        "ident_bf": np.eye(128, dtype=BFNP),
        "ident_f32": np.eye(128, dtype=np.float32),
        "ones_r_bf": np.ones((1, 128), dtype=BFNP),
        "ones_r_f": np.ones((1, 128), dtype=np.float32),
        "ones_c_f": np.ones((128, 1), dtype=np.float32),
        "sel32": sel,
    }
    x = np.ascontiguousarray(L0_attn_x)
    in_maps = [dict(common, x=np.ascontiguousarray(x[b])) for b in range(8)]
    res = run_bass_kernel_spmd(nc, in_maps, core_ids=list(range(8)))
    out = np.stack([res.results[b]["out"] for b in range(8)], axis=0)
    kernel._results = res.results
    return out
